# version 03
# baseline (speedup 1.0000x reference)
import numpy as np
import ml_dtypes
import concourse.bass as bass
import concourse.mybir as mybir
from concourse.bass_utils import run_bass_kernel_spmd
from contextlib import ExitStack

F32 = mybir.dt.float32
BF16 = mybir.dt.bfloat16
ALU = mybir.AluOpType
AF = mybir.ActivationFunctionType
AX = mybir.AxisListType

B, T, C, NI = 4, 2048, 2048, 8192
NT = 1024
NCT = C // 128
NG = NI // 128
NMOD = 6 * C // 128
EPS = 1e-5
C23 = 12582912.0
KFLAG = float((1.5 * 2 ** 52 - C23) * 2 ** 30)
NWM = C // 8
CNORM = float(np.float32(1.0 / np.sqrt(8192.0)))

_DEBUG = False
_DEBUG2 = False
_UPTO = None


class _Stop(Exception):
    pass


def _build():
    nc = bass.Bass("TRN2", target_bir_lowering=False, debug=False)

    d_xT = nc.dram_tensor("xT", [C, NT], F32, kind="ExternalInput")
    d_wvT = nc.dram_tensor("wvT", [C, C], F32, kind="ExternalInput")
    d_wrT = nc.dram_tensor("wrT", [C, C], F32, kind="ExternalInput")
    d_wkT = nc.dram_tensor("wkT", [C, C], F32, kind="ExternalInput")
    d_woT = nc.dram_tensor("woT", [C, C], F32, kind="ExternalInput")
    d_wgT = nc.dram_tensor("wgT", [C, NI], F32, kind="ExternalInput")
    d_wuT = nc.dram_tensor("wuT", [C, NI], F32, kind="ExternalInput")
    d_wdT = nc.dram_tensor("wdT", [NI, C], F32, kind="ExternalInput")
    d_wmod = nc.dram_tensor("wmod_sl", [NWM, 6 * C], F32, kind="ExternalInput")
    d_cT = nc.dram_tensor("cT_sl", [NWM, B], F32, kind="ExternalInput")
    d_pv = nc.dram_tensor("pvecs", [128, 128], F32, kind="ExternalInput")
    d_cb = nc.dram_tensor("constb", [128, 385], BF16, kind="ExternalInput")
    d_cf = nc.dram_tensor("constf", [128, 128], F32, kind="ExternalInput")
    d_or = nc.dram_tensor("onesrow", [1, 128], F32, kind="ExternalInput")
    d_mask = nc.dram_tensor("maskf", [128, NMOD * 4], F32, kind="ExternalInput")
    d_msg = nc.dram_tensor("wms_g", [C, 1024], F32, kind="ExternalInput")
    d_msu = nc.dram_tensor("wms_u", [C, 1024], F32, kind="ExternalInput")
    d_msd = nc.dram_tensor("wms_d", [NI, 256], F32, kind="ExternalInput")
    d_out = nc.dram_tensor("outT", [C, NT], F32, kind="ExternalOutput")

    d_x2 = nc.dram_tensor("x2_scratch", [C, NT], F32, kind="Internal")
    d_ccmi = nc.dram_tensor("cc_mod_in", [128, NMOD * 4 + 4], F32, kind="Internal")
    d_ccmo = nc.dram_tensor("cc_mod_out", [128, NMOD * 4 + 4], F32, kind="Internal")

    dbg2 = {}
    if _DEBUG2:
        for nm, shp in [("dbg2_kc", [128, 96]), ("dbg2_pcol", [128, 176]),
                        ("dbg2_mcols", [128, 8]), ("dbg2_srow", [1, 8])]:
            dbg2[nm] = nc.dram_tensor(nm, shp, F32, kind="ExternalOutput")

    dbg = {}
    if _DEBUG:
        for nm, shp in [("dbg_mod", [128, 96]), ("dbg_xv", [128, NCT * NT]),
                        ("dbg_m", [128, NCT * NT]), ("dbg_h2q", [128, NCT * NT]),
                        ("dbg_z", [128, 16 * NT]), ("dbg_xq", [128, 16 * NT]),
                        ("dbg_cols", [128, 192]), ("dbg_rows", [1, 4096])]:
            dbg[nm] = nc.dram_tensor(nm, shp, F32, kind="ExternalOutput")

    with ExitStack() as stk:
        e = stk.enter_context
        zbuf = e(nc.sbuf_tensor("zbuf", [128, NG * NT], BF16))
        h2q = e(nc.sbuf_tensor("h2q", [128, NCT * NT], BF16))
        u2s = e(nc.sbuf_tensor("u2s", [128, NT], F32))
        xcol = e(nc.sbuf_tensor("xcol", [128, 2 * NT], F32))
        bcA = e(nc.sbuf_tensor("bcA", [128, NT], F32))
        bcB = e(nc.sbuf_tensor("bcB", [128, NT], F32))
        wf = e(nc.sbuf_tensor("wf", [128, 2 * 1024], F32))
        wq = e(nc.sbuf_tensor("wq", [128, 2 * 2048], BF16))
        vt = e(nc.sbuf_tensor("vt", [128, 2 * NT], BF16))
        pv = e(nc.sbuf_tensor("pv", [128, 128], F32))
        cstb = e(nc.sbuf_tensor("cstb", [128, 385], BF16))
        cstf = e(nc.sbuf_tensor("cstf", [128, 128], F32))
        orow = e(nc.sbuf_tensor("orow", [1, 128], F32))
        pcol = e(nc.sbuf_tensor("pcol", [128, 176], F32))
        scol = e(nc.sbuf_tensor("scol", [128, 4], F32))
        mcols = e(nc.sbuf_tensor("mcols", [128, 8], F32))
        kc = e(nc.sbuf_tensor("kc", [128, 96], F32))
        xk0b = e(nc.sbuf_tensor("xk0b", [128, 16], BF16))
        msel = e(nc.sbuf_tensor("msel", [128, 96], F32))
        srow = e(nc.sbuf_tensor("srow", [1, 8], F32))
        csb = e(nc.sbuf_tensor("csb", [128, 8], F32))
        silc = e(nc.sbuf_tensor("silc", [128, 8], BF16))
        pb = [e(nc.psum_tensor(f"pb{i}", [128, 512], F32)) for i in range(8)]
        armod = xcol[:, 0:384]
        modst = xcol[:, 384:768]
        mxa = h2q[:, 1536:2560]
        w2b = h2q[:, 2560:3584]
        xbs = h2q[:, 4096:6144]
        sc1b = wq[:, 0:1024]
        rows = h2q[:, 6144:12288].bitcast(F32)[0:1, :]
        sHD = e(nc.semaphore("sHD"))
        sGH = e(nc.semaphore("sGH"))
        sCC = e(nc.semaphore("sCC"))
        sB = [e(nc.semaphore("sB0")), e(nc.semaphore("sB1"))]
        sA = [e(nc.semaphore("sA0")), e(nc.semaphore("sA1"))]
        sW = [e(nc.semaphore("sW0")), e(nc.semaphore("sW1"))]
        sX = [e(nc.semaphore("sX0")), e(nc.semaphore("sX1"))]
        sD = e(nc.semaphore("sD"))
        sT = e(nc.semaphore("sT"))
        sV = e(nc.semaphore("sV"))
        sS = e(nc.semaphore("sS"))
        sO = e(nc.semaphore("sO"))
        sC = e(nc.semaphore("sC"))
        block = e(nc.Block())

        sched = {k: [] for k in ("sync", "gpsimd", "tensor", "vector", "scalar")}
        cnt = {"HD": 0, "GH": 0, "CC": 0, "B0": 0, "B1": 0, "A0": 0, "A1": 0,
               "W0": 0, "W1": 0, "X0": 0, "X1": 0, "D": 0,
               "T": 0, "V": 0, "S": 0, "O": 0, "C": 0}
        SEM = {"HD": sHD, "GH": sGH, "CC": sCC, "B0": sB[0], "B1": sB[1],
               "A0": sA[0], "A1": sA[1], "W0": sW[0], "W1": sW[1],
               "X0": sX[0], "X1": sX[1], "D": sD,
               "T": sT, "V": sV, "S": sS, "O": sO, "C": sC}

        def em(eng, f):
            sched[eng].append(f)

        def wait(eng, key, val):
            if val and val > 0:
                em(eng, lambda en, s=SEM[key], v=val: en.wait_ge(s, v))

        def bump(key, k=1):
            cnt[key] += k
            return cnt[key]

        def flush(eng, key):
            v = mark(eng, key)
            wait(eng, key, v)

        def vchain(ops):
            for f in ops:
                em("vector", f)
                flush("vector", "V")

        def mark(eng, key):
            last = sched[eng].pop()

            def wrapped(en, f=last, sm=SEM[key]):
                ins = f(en)
                ins.then_inc(sm, 1)
            sched[eng].append(wrapped)
            return bump(key)

        def ckpt(k):
            if _UPTO is not None and _UPTO <= k:
                raise _Stop()

        XV0 = 0
        M0 = 20 * NT
        XR0 = 40 * NT

        identb = cstb[:, 0:128]
        h128b = cstb[:, 128:256]
        bmatb = cstb[:, 256:384]
        onescol_b = cstb[:, 384:385]
        identf = cstf[:, 0:128]

        tfc, tmkc, tmvc, tmrc, xxc, aac, bbc, ppc = [pv[:, 16 * j:16 * (j + 1)] for j in range(8)]
        AV, BV, AR, BR, AK, BK, GA2, GB2, CB1, SBC, GBD = [pcol[:, 16 * j:16 * (j + 1)] for j in range(11)]
        cm_col = scol[:, 0:1]
        fl_col = scol[:, 1:2]
        ukcol = kc[:, 0:16]
        kcolv = kc[:, 32:48]
        v0c = kc[:, 48:64]
        r0c = kc[:, 64:80]
        t0c = kc[:, 80:96]
        r_m = rows[:, 0:1024]
        r_a = rows[:, 1024:2048]
        r_b = rows[:, 2048:3072]
        r_rstd = r_b

        try:
            em("sync", lambda en: en.dma_start(cstb[:, :], d_cb[:, :]).then_inc(sHD, 16)); bump("HD", 16)
            em("sync", lambda en: en.dma_start(cstf[:, :], d_cf[:, :]).then_inc(sHD, 16)); bump("HD", 16)
            em("sync", lambda en: en.dma_start(orow[:, :], d_or[:, :]).then_inc(sHD, 16)); bump("HD", 16)
            em("sync", lambda en: en.dma_start(pv[:, :], d_pv[:, :]).then_inc(sHD, 16)); bump("HD", 16)
            em("sync", lambda en: en.dma_start(armod[:, :], d_mask[:, :]).then_inc(sHD, 16)); bump("HD", 16)
            em("sync", lambda en: en.dma_start(
                csb[:, :], bass.AP(d_cT, 0, [[B, 128], [128 * B, 2], [1, B]])).then_inc(sHD, 16))
            HD_all = bump("HD", 16)
            L_cstb = L_orow = L_pv = L_mask = L_csb = HD_all
            em("gpsimd", lambda en: en.dma_start(
                bass.AP(zbuf, 0, [[NG * NT, 128], [6 * C, 2], [1, 6 * C]]),
                bass.AP(d_wmod, 0, [[6 * C, 128], [6 * C * 128, 2], [1, 6 * C]])).then_inc(sGH, 16))
            G_wmod = bump("GH", 16)

            ckpt(0.2)
            wait("scalar", "HD", L_csb)
            em("scalar", lambda en: en.activation(silc[:, :], csb[:, :], AF.Silu))
            S_silc = mark("scalar", "S")

            wait("vector", "HD", L_cstb)
            vchain([lambda en: en.memset(fl_col, 2.0 ** -30),
                    lambda en: en.tensor_scalar(fl_col, fl_col, 1.0, 1.0, ALU.add, ALU.subtract)])
            em("vector", lambda en: en.tensor_scalar(cm_col, fl_col, KFLAG, C23, ALU.mult, ALU.add))
            V_cmagic = mark("vector", "V")

            ckpt(0.3)
            wait("tensor", "GH", G_wmod)
            wait("tensor", "S", S_silc)
            for ot in range(NMOD):
                for cs in range(2):
                    em("tensor", lambda en, o=ot, c=cs, st=(ot == 0 and cs == 0),
                       sp=(ot == NMOD - 1 and cs == 1): en.matmul(
                        pb[6][:, 4 * o:4 * (o + 1)],
                        zbuf[:, c * 6 * C + 128 * o: c * 6 * C + 128 * o + 128],
                        silc[:, 4 * c:4 * (c + 1)],
                        start=st, stop=sp))
            T_mod = mark("tensor", "T")
            ckpt(0.4)
            wait("vector", "T", T_mod)
            em("vector", lambda en: en.tensor_copy(modst[:, :], pb[6][:, 0:NMOD * 4]))
            V_modev = mark("vector", "V")
            ckpt(0.5)
            mean_parts = [(d_msg, 0), (d_msu, 32), (d_msd, 64)]
            V_mb = {}
            mbi = 0
            for wdram, prow in mean_parts:
                nblk = 16
                for blk in range(nblk):
                    if mbi >= 2:
                        wait("sync", "T", V_mb[mbi - 2])
                    if wdram is d_msd:
                        srcap = bass.AP(wdram, blk * 4 * 128 * 256, [[256, 128], [128 * 256, 4], [1, 256]])
                    else:
                        srcap = bass.AP(wdram, blk * 128 * 1024, [[1024, 128], [1, 1024]])
                    em("sync", lambda en, sap=srcap, i=mbi % 2, sm=sW[mbi % 2]: en.dma_start(
                        wf[:, 1024 * i:1024 * (i + 1)], sap).then_inc(sm, 16))
                    Lb = bump(f"W{mbi % 2}", 16)
                    wait("scalar", f"W{mbi % 2}", Lb)
                    em("scalar", lambda en, i=mbi % 2: en.activation(
                        wq[:, 2048 * i:2048 * i + 1024], wf[:, 1024 * i:1024 * (i + 1)], AF.Abs))
                    Vb = mark("scalar", "S")
                    wait("tensor", "S", Vb)
                    for ch in range(2):
                        em("tensor", lambda en, i=mbi % 2, c=ch, pr=prow,
                           st=(blk == 0 and ch == 0), sp=(blk == nblk - 1 and ch == 1): en.matmul(
                            pb[7][pr:pr + 1, 0:512], onescol_b,
                            wq[:, 2048 * i + 512 * c: 2048 * i + 512 * (c + 1)],
                            start=st, stop=sp))
                    Tb = mark("tensor", "T")
                    V_mb[mbi] = Tb
                    mbi += 1
            T_mean = cnt["T"]
            wait("vector", "T", T_mean)
            em("vector", lambda en: en.reduce_sum(srow[:, 0:1], pb[7][0:1, 0:512], AX.X))
            em("vector", lambda en: en.reduce_sum(srow[:, 1:2], pb[7][32:33, 0:512], AX.X))
            em("vector", lambda en: en.reduce_sum(srow[:, 2:3], pb[7][64:65, 0:512], AX.X))
            em("vector", lambda en: en.memset(srow[:, 3:4], 0.0))
            V_sums = mark("vector", "V")
            wait("gpsimd", "V", V_sums)
            em("gpsimd", lambda en: en.dma_start(d_ccmi[:, 0:NMOD * 4], modst[:, :]).then_inc(sCC, 16)); bump("CC", 16)
            em("gpsimd", lambda en: en.dma_start(
                d_ccmi[0:1, NMOD * 4:NMOD * 4 + 4], srow[:, 0:4]).then_inc(sCC, 16)); G_mi = bump("CC", 16)
            wait("gpsimd", "CC", G_mi)
            em("gpsimd", lambda en: en.collective_compute(
                "AllReduce", ALU.add, replica_groups=[[0, 1, 2, 3, 4, 5, 6, 7]],
                ins=[d_ccmi[:, :]], outs=[d_ccmo[:, :]]).then_inc(sC, 1))
            C_mod = bump("C")
            wait("gpsimd", "C", C_mod)
            em("gpsimd", lambda en: en.dma_start(modst[:, :], d_ccmo[:, 0:NMOD * 4]).then_inc(sCC, 16)); bump("CC", 16)
            em("gpsimd", lambda en: en.dma_start(
                srow[:, 4:8], d_ccmo[0:1, NMOD * 4:NMOD * 4 + 4]).then_inc(sCC, 16)); G_mo = bump("CC", 16)
            ckpt(0.8)
            wait("vector", "CC", G_mo)
            wait("vector", "HD", L_mask)
            em("vector", lambda en: en.tensor_mul(modst[:, :], modst[:, :], armod[:, :]))
            em("vector", lambda en: en.reduce_sum(
                msel[:, :], modst.rearrange("p (a b) -> p a b", b=4), AX.X))
            V_msel = mark("vector", "V")

            sa_c = msel[:, 0:16]; ca_c = msel[:, 16:32]; ga_c = msel[:, 32:48]
            sb_c = msel[:, 48:64]; cb_c = msel[:, 64:80]; gb_c = msel[:, 80:96]

            wait("vector", "HD", L_pv)
            wait("vector", "V", V_msel)

            vchain([
                lambda en: en.tensor_scalar(CB1, ca_c, 1.0, None, ALU.add),
                lambda en: en.tensor_mul(AV, CB1, tmvc),
                lambda en: en.tensor_mul(AR, CB1, tmrc),
                lambda en: en.tensor_mul(AK, CB1, tmkc),
                lambda en: en.tensor_mul(BV, sa_c, tmvc),
                lambda en: en.tensor_scalar(SBC, tmvc, -1.0, 1.0, ALU.mult, ALU.add),
                lambda en: en.tensor_mul(SBC, SBC, xxc),
                lambda en: en.tensor_add(BV, BV, SBC),
                lambda en: en.tensor_mul(BR, sa_c, tmrc),
                lambda en: en.tensor_scalar(SBC, tmrc, -1.0, 1.0, ALU.mult, ALU.add),
                lambda en: en.tensor_mul(SBC, SBC, xxc),
                lambda en: en.tensor_add(BR, BR, SBC),
                lambda en: en.tensor_mul(BK, sa_c, tmkc),
                lambda en: en.tensor_scalar(SBC, tmkc, -1.0, 1.0, ALU.mult, ALU.add),
                lambda en: en.tensor_mul(SBC, SBC, xxc),
                lambda en: en.tensor_add(BK, BK, SBC),
                lambda en: en.scalar_tensor_tensor(GA2, ga_c, -1.0, ga_c, ALU.mult, ALU.max),
                lambda en: en.tensor_scalar(GA2, GA2, 0.25, None, ALU.max),
                lambda en: en.scalar_tensor_tensor(GB2, gb_c, -1.0, gb_c, ALU.mult, ALU.max),
                lambda en: en.tensor_scalar(GB2, GB2, 0.25, None, ALU.max),
                lambda en: en.tensor_scalar(CB1, cb_c, 1.0, None, ALU.add),
                lambda en: en.tensor_copy(SBC, sb_c),
            ])
            V_dcols = cnt["V"]
            wait("scalar", "V", V_dcols)
            em("scalar", lambda en: en.activation(t0c, ga_c, AF.Sign))
            S_sga = mark("scalar", "S")
            wait("vector", "S", S_sga)
            em("vector", lambda en: en.tensor_mul(GA2, GA2, t0c))
            V_ga = mark("vector", "V")
            wait("scalar", "V", V_ga)
            em("scalar", lambda en: en.activation(t0c, gb_c, AF.Sign))
            S_sgb = mark("scalar", "S")
            wait("vector", "S", S_sgb)
            em("vector", lambda en: en.tensor_mul(GB2, GB2, t0c))
            V_cols = mark("vector", "V")

            ckpt(1)
            wait("vector", "CC", G_mo)
            vchain([lambda en: en.tensor_scalar(srow[:, 4:8], srow[:, 4:8], 1.0 / (C * NI), EPS, ALU.mult, ALU.max)])
            em("vector", lambda en: en.reciprocal(srow[:, 0:4], srow[:, 4:8]))
            V_means = mark("vector", "V")
            wait("tensor", "V", V_means)
            wait("tensor", "HD", L_orow)
            em("tensor", lambda en: en.matmul(pb[4][:, 0:4], orow[:, :], srow[:, 4:8], start=True, stop=True))
            em("tensor", lambda en: en.matmul(pb[4][:, 4:8], orow[:, :], srow[:, 0:4], start=True, stop=True))
            T_mc = mark("tensor", "T")
            wait("vector", "T", T_mc)
            em("vector", lambda en: en.tensor_copy(mcols[:, :], pb[4][:, 0:8]))
            V_mcols = mark("vector", "V")
            wait("scalar", "V", V_mcols)
            em("scalar", lambda en: en.activation(GBD, GB2, AF.Identity, scale=mcols[:, 2:3]))
            S_gbd = mark("scalar", "S")

            ckpt(2)
            xb_T = {-2: 0, -1: 0}
            for ct in range(NCT):
                wait("gpsimd", "T", xb_T[ct - 2])
                em("gpsimd", lambda en, i=ct % 2, c=ct, sm=sB[ct % 2]: en.dma_start(
                    xbs[:, NT * i:NT * (i + 1)],
                    bass.AP(d_xT, c * 128 * NT, [[NT, 128], [1, NT]])).then_inc(sm, 16))
                Gx = bump(f"B{ct % 2}", 16)
                wait("scalar", f"B{ct % 2}", Gx)
                em("scalar", lambda en, i=ct % 2: en.activation(
                    vt[:, NT * i:NT * (i + 1)], xbs[:, NT * i:NT * (i + 1)], AF.Square))
                Ss = mark("scalar", "S")
                wait("tensor", "S", Ss)
                for ch in range(2):
                    em("tensor", lambda en, i=ct % 2, c=ch, st=(ct == 0), sp=(ct == NCT - 1): en.matmul(
                        pb[5][32 * c:32 * c + 1, 0:512], onescol_b,
                        xbs[:, NT * i + 512 * c:NT * i + 512 * (c + 1)], start=st, stop=sp))
                    em("tensor", lambda en, i=ct % 2, c=ch, st=(ct == 0), sp=(ct == NCT - 1): en.matmul(
                        pb[5 if c == 0 else 2][64 * (1 - c):64 * (1 - c) + 1, 0:512], onescol_b,
                        vt[:, NT * i + 512 * c:NT * i + 512 * (c + 1)], start=st, stop=sp))
                xb_T[ct] = mark("tensor", "T")

            def ln_rows():
                vchain([
                    lambda en: en.tensor_scalar(r_m[:, 0:512], pb[5][0:1, 0:512], 1.0 / C, None, ALU.mult),
                    lambda en: en.tensor_scalar(r_m[:, 512:1024], pb[5][32:33, 0:512], 1.0 / C, None, ALU.mult),
                    lambda en: en.tensor_scalar(r_a[:, 0:512], pb[5][64:65, 0:512], 1.0 / C, None, ALU.mult),
                    lambda en: en.tensor_scalar(r_a[:, 512:1024], pb[2][0:1, 0:512], 1.0 / C, None, ALU.mult),
                    lambda en: en.tensor_mul(r_b, r_m, r_m),
                    lambda en: en.tensor_sub(r_a, r_a, r_b),
                ])
                em("vector", lambda en: en.tensor_scalar(r_a, r_a, EPS, None, ALU.add))
                return mark("vector", "V")
            wait("vector", "T", xb_T[NCT - 1])
            V_ln1a = ln_rows()
            wait("scalar", "V", V_ln1a)
            em("scalar", lambda en: en.activation(r_b, r_a, AF.Sqrt))
            S_sq1 = mark("scalar", "S")
            wait("vector", "S", S_sq1)
            em("vector", lambda en: en.reciprocal(r_b, r_b))
            V_ln1 = mark("vector", "V")

            def bcast(row_ap, dst, tw, marker_prev=None):
                wait("tensor", "V", tw)
                last = None
                for ch in range(2):
                    em("tensor", lambda en, c=ch, r=row_ap: en.matmul(
                        pb[4][:, 0:512], orow[:, :], r[:, 512 * c:512 * (c + 1)], start=True, stop=True))
                    Tb = mark("tensor", "T")
                    wait("vector", "T", Tb)
                    em("vector", lambda en, c=ch, d=dst: en.tensor_copy(d[:, 512 * c:512 * (c + 1)], pb[4][:, 0:512]))
                    last = mark("vector", "V")
                    if ch == 0:
                        wait("tensor", "V", last)
                return last

            bcast(r_m, bcA, V_ln1)
            V_bc1 = bcast(r_rstd, bcB, cnt["V"])

            ckpt(3)
            ap_T = {-2: xb_T[NCT - 2], -1: xb_T[NCT - 1]}
            S_ap = {}
            for ct in range(NCT):
                wait("gpsimd", "S", S_ap.get(ct - 2, 0))
                if ct < 2:
                    wait("gpsimd", "T", xb_T[NCT - 2 + ct])
                em("gpsimd", lambda en, i=ct % 2, c=ct, sm=sB[ct % 2]: en.dma_start(
                    xbs[:, NT * i:NT * (i + 1)],
                    bass.AP(d_xT, c * 128 * NT, [[NT, 128], [1, NT]])).then_inc(sm, 16))
                Gx = bump(f"B{ct % 2}", 16)
                wait("vector", f"B{ct % 2}", Gx)
                if ct > 0:
                    wait("vector", "S", S_ap[ct - 1])
                em("vector", lambda en, i=ct % 2: en.tensor_sub(u2s[:, :], xbs[:, NT * i:NT * (i + 1)], bcA[:, :]))
                em("vector", lambda en: en.tensor_mul(u2s[:, :], u2s[:, :], bcB[:, :]))
                em("vector", lambda en, c=ct: en.tensor_copy(ukcol[:, c:c + 1], u2s[:, 0:1]))
                Vu = mark("vector", "V")
                wait("scalar", "V", Vu)
                if ct == 0:
                    wait("scalar", "V", V_cols)
                em("scalar", lambda en, c=ct: en.activation(
                    zbuf[:, XV0 + NT * c:XV0 + NT * (c + 1)], u2s[:, :], AF.Identity,
                    bias=BV[:, c:c + 1], scale=AV[:, c:c + 1]))
                em("scalar", lambda en, c=ct: en.activation(
                    zbuf[:, XR0 + NT * c:XR0 + NT * (c + 1)], u2s[:, :], AF.Identity,
                    bias=BR[:, c:c + 1], scale=AR[:, c:c + 1]))
                S_ap[ct] = mark("scalar", "S")
            S_ln1apply = cnt["S"]
            wait("vector", "S", S_ln1apply)
            vchain([lambda en: en.tensor_mul(t0c, ukcol, AK),
                    lambda en: en.tensor_add(t0c, t0c, BK)])
            em("vector", lambda en: en.tensor_copy(xk0b[:, :], t0c))
            V_xk0 = mark("vector", "V")

            ckpt(4)
            blk_list = []
            for ot in range(NCT):
                blk_list += [(d_wvT, ot), (d_wrT, ot)]
            blk_list += [(d_wkT, ot) for ot in range(NCT)]
            blk_list += [(d_woT, ot) for ot in range(NCT)]

            def wblk_ap(dram, ot):
                return bass.AP(dram, ot * 128, [[C, 128], [128 * C, NCT], [1, 128]])

            blk_T = {}
            G_blk = {}

            def emit_blk_dma(bi):
                if bi >= len(blk_list):
                    return
                dram, ot = blk_list[bi]
                if bi < 2:
                    wait("gpsimd", "T", T_mean)
                else:
                    wait("gpsimd", "T", blk_T[bi - 2])
                em("gpsimd", lambda en, i=bi % 2, d=dram, o=ot, sm=sA[bi % 2]: en.dma_start(
                    wq[:, 2048 * i:2048 * (i + 1)], wblk_ap(d, o)).then_inc(sm, 16))
                G_blk[bi] = (f"A{bi % 2}", bump(f"A{bi % 2}", 16))

            emit_blk_dma(0)
            emit_blk_dma(1)

            bi = 0
            for ot in range(NCT):
                for kind, pbase, src0 in (("v", 0, XV0), ("r", 2, XR0)):
                    wait("tensor", G_blk[bi][0], G_blk[bi][1])
                    if bi == 0:
                        wait("tensor", "S", S_ln1apply)
                    for ch in range(2):
                        for ct in range(NCT):
                            em("tensor", lambda en, i=bi % 2, c=ct, cc=ch, pp=pbase, s0=src0: en.matmul(
                                pb[pp + cc][:, 0:512],
                                wq[:, 2048 * i + 128 * c:2048 * i + 128 * (c + 1)],
                                zbuf[:, s0 + NT * c + 512 * cc:s0 + NT * c + 512 * (cc + 1)],
                                start=(c == 0), stop=(c == NCT - 1)))
                    blk_T[bi] = mark("tensor", "T")
                    bi += 1
                    emit_blk_dma(bi + 1)
                wait("scalar", "T", blk_T[bi - 2])
                em("scalar", lambda en: en.activation(vt[:, 0:512], pb[0][:, 0:512], AF.Identity))
                em("scalar", lambda en: en.activation(vt[:, 512:1024], pb[1][:, 0:512], AF.Identity))
                S_v = mark("scalar", "S")
                wait("scalar", "T", blk_T[bi - 1])
                em("scalar", lambda en: en.activation(vt[:, NT:NT + 512], pb[2][:, 0:512], AF.Sigmoid))
                em("scalar", lambda en: en.activation(vt[:, NT + 512:2 * NT], pb[3][:, 0:512], AF.Sigmoid))
                S_r = mark("scalar", "S")
                wait("vector", "S", S_r)
                em("vector", lambda en, o=ot: en.tensor_copy(v0c[:, o:o + 1], vt[:, 0:1]))
                em("vector", lambda en, o=ot: en.tensor_copy(r0c[:, o:o + 1], vt[:, NT:NT + 1]))
                em("vector", lambda en, o=ot: en.tensor_mul(
                    zbuf[:, M0 + NT * o:M0 + NT * (o + 1)], vt[:, 0:NT], vt[:, NT:2 * NT]))
                V_m = mark("vector", "V")
                wait("tensor", "S", S_r)
                wait("tensor", "V", V_m)

            for ot in range(NCT):
                bidx = 32 + ot
                wait("tensor", G_blk[bidx][0], G_blk[bidx][1])
                if ot == 0:
                    wait("tensor", "V", V_xk0)
                for ct in range(NCT):
                    em("tensor", lambda en, i=bidx % 2, c=ct, o=ot: en.matmul(
                        pb[6][:, o:o + 1],
                        wq[:, 2048 * i + 128 * c:2048 * i + 128 * (c + 1)],
                        xk0b[:, c:c + 1],
                        start=(c == 0), stop=(c == NCT - 1)))
                blk_T[bidx] = mark("tensor", "T")
                emit_blk_dma(bidx + 2)
            T_kcol = cnt["T"]
            wait("vector", "T", T_kcol)
            vchain([lambda en: en.tensor_copy(kcolv, pb[6][:, 0:16]),
                    lambda en: en.tensor_add(kcolv, kcolv, tfc),
                    lambda en: en.tensor_max(t0c, kcolv, ppc),
                    lambda en: en.tensor_sub(kcolv, kcolv, t0c)])
            em("vector", lambda en: en.tensor_sub(t0c, ppc, t0c))
            V_pre = mark("vector", "V")
            wait("scalar", "V", V_pre)
            KE1 = kc[:, 0:16]
            em("scalar", lambda en: en.activation(KE1, t0c, AF.Exp))
            flush("scalar", "S")
            em("scalar", lambda en: en.activation(t0c, kcolv, AF.Exp))
            S_exp = mark("scalar", "S")
            wait("vector", "S", S_exp)
            KN = kc[:, 16:32]
            vchain([lambda en: en.tensor_mul(KN, KE1, aac),
                    lambda en: en.tensor_mul(kcolv, t0c, v0c),
                    lambda en: en.tensor_add(KN, KN, kcolv),
                    lambda en: en.tensor_mul(kcolv, KE1, bbc),
                    lambda en: en.tensor_add(kcolv, kcolv, t0c),
                    lambda en: en.reciprocal(kcolv, kcolv),
                    lambda en: en.tensor_mul(KN, KN, kcolv),
                    lambda en: en.tensor_mul(KN, KN, r0c)])
            em("vector", lambda en: en.tensor_copy(bass.AP(zbuf, M0, [[NG * NT, 128], [NT, 16]]), KN))
            V_m0 = mark("vector", "V")

            xr_S = {-2: 0, -1: 0}
            for ot in range(NCT):
                wait("sync", "S", xr_S[ot - 2])
                if ot == 0:
                    wait("sync", "V", V_msel)
                em("sync", lambda en, i=ot % 2, o=ot, sm=sX[ot % 2]: en.dma_start(
                    xcol[:, NT * i:NT * (i + 1)],
                    bass.AP(d_xT, o * 128 * NT, [[NT, 128], [1, NT]])).then_inc(sm, 16))
                Lx = (f"X{ot % 2}", bump(f"X{ot % 2}", 16))
                bidx = 48 + ot
                wait("tensor", G_blk[bidx][0], G_blk[bidx][1])
                if ot == 0:
                    wait("tensor", "V", V_m0)
                for ch in range(2):
                    for mt in range(NCT):
                        em("tensor", lambda en, i=bidx % 2, m_=mt, cc=ch: en.matmul(
                            pb[cc][:, 0:512],
                            wq[:, 2048 * i + 128 * m_:2048 * i + 128 * (m_ + 1)],
                            zbuf[:, M0 + NT * m_ + 512 * cc:M0 + NT * m_ + 512 * (cc + 1)],
                            start=(m_ == 0), stop=(m_ == NCT - 1)))
                blk_T[bidx] = mark("tensor", "T")
                emit_blk_dma(bidx + 2)
                wait("vector", "T", blk_T[bidx])
                wait("vector", Lx[0], Lx[1])
                for ch in range(2):
                    em("vector", lambda en, i=ot % 2, cc=ch, o=ot: en.scalar_tensor_tensor(
                        xcol[:, NT * i + 512 * cc:NT * i + 512 * (cc + 1)],
                        pb[cc][:, 0:512], GA2[:, o:o + 1],
                        xcol[:, NT * i + 512 * cc:NT * i + 512 * (cc + 1)],
                        ALU.mult, ALU.add))
                V_x2 = mark("vector", "V")
                wait("tensor", "V", V_x2)
                wait("scalar", "V", V_x2)
                em("scalar", lambda en, i=ot % 2: en.activation(vt[:, 0:NT], xcol[:, NT * i:NT * (i + 1)], AF.Identity))
                em("scalar", lambda en: en.activation(vt[:, NT:2 * NT], vt[:, 0:NT], AF.Square))
                S_x2c = mark("scalar", "S")
                xr_S[ot] = S_x2c
                wait("tensor", "S", S_x2c)
                for ch in range(2):
                    em("tensor", lambda en, c=ch, st=(ot == 0), sp=(ot == NCT - 1): en.matmul(
                        pb[5][32 * c:32 * c + 1, 0:512], onescol_b,
                        vt[:, 512 * c:512 * (c + 1)], start=st, stop=sp))
                    em("tensor", lambda en, c=ch, st=(ot == 0), sp=(ot == NCT - 1): en.matmul(
                        pb[5 if c == 0 else 2][64 * (1 - c):64 * (1 - c) + 1, 0:512], onescol_b,
                        vt[:, NT + 512 * c:NT + 512 * (c + 1)], start=st, stop=sp))
                T_s2 = mark("tensor", "T")
                wait("sync", "V", V_x2)
                em("sync", lambda en, i=ot % 2, o=ot: en.dma_start(
                    bass.AP(d_x2, o * 128 * NT, [[NT, 128], [1, NT]]),
                    xcol[:, NT * i:NT * (i + 1)]).then_inc(sO, 16))
                bump("O", 16)
                wait("scalar", "T", T_s2)
            T_ln2stats = cnt["T"]
            O_x2 = cnt["O"]

            if _DEBUG:
                wait("gpsimd", "V", V_m0)
                em("gpsimd", lambda en: en.dma_start(dbg["dbg_m"][:, :], zbuf[:, M0:M0 + NCT * NT]).then_inc(sD, 16))
                bump("D", 16)
                em("gpsimd", lambda en: en.dma_start(dbg["dbg_xv"][:, :], zbuf[:, XV0:XV0 + NCT * NT]).then_inc(sD, 16))
                bump("D", 16)
                em("gpsimd", lambda en: en.dma_start(dbg["dbg_mod"][:, :], msel[:, :]).then_inc(sD, 16))
                bump("D", 16)
                em("gpsimd", lambda en: en.dma_start(dbg["dbg_cols"][:, 0:176], pcol[:, :]).then_inc(sD, 16))
                bump("D", 16)

            wait("vector", "T", T_ln2stats)
            V_ln2a = ln_rows()
            wait("scalar", "V", V_ln2a)
            em("scalar", lambda en: en.activation(r_b, r_a, AF.Sqrt))
            S_sq2 = mark("scalar", "S")
            wait("vector", "S", S_sq2)
            em("vector", lambda en: en.reciprocal(r_b, r_b))
            V_ln2 = mark("vector", "V")
            bcast(r_m, bcA, V_ln2)
            V_bc2 = bcast(r_rstd, bcB, cnt["V"])

            ckpt(5)
            x2_V = {-2: 0, -1: 0}
            for p in range(2):
                for ct in range(NCT):
                    idx = p * NCT + ct
                    wait("sync", "S", x2_V[idx - 2])
                    if idx < 2:
                        wait("sync", "S", xr_S[NCT - 2 + idx])
                    if idx == 0:
                        wait("sync", "O", O_x2)
                    em("sync", lambda en, i=idx % 2, o=ct, sm=sX[idx % 2]: en.dma_start(
                        xcol[:, NT * i:NT * (i + 1)],
                        bass.AP(d_x2, o * 128 * NT, [[NT, 128], [1, NT]])).then_inc(sm, 16))
                    Lx = bump(f"X{idx % 2}", 16)
                    wait("vector", f"X{idx % 2}", Lx)
                    em("vector", lambda en, i=idx % 2: en.tensor_sub(u2s[:, :], xcol[:, NT * i:NT * (i + 1)], bcA[:, :]))
                    em("vector", lambda en: en.tensor_mul(u2s[:, :], u2s[:, :], bcB[:, :]))
                    Vu = mark("vector", "V")
                    wait("scalar", "V", Vu)
                    em("scalar", lambda en, c=ct, fn=(AF.Abs if p == 0 else AF.Identity): en.activation(
                        vt[:, 0:NT], u2s[:, :], fn, bias=SBC[:, c:c + 1], scale=CB1[:, c:c + 1]))
                    Sh = mark("scalar", "S")
                    x2_V[idx] = Sh
                    wait("vector", "S", Sh)
                    if p == 0:
                        if ct == 0:
                            em("vector", lambda en: en.tensor_copy(mxa[:, :], vt[:, 0:NT]))
                        else:
                            em("vector", lambda en: en.tensor_max(mxa[:, :], mxa[:, :], vt[:, 0:NT]))
                        mark("vector", "V")
                    else:
                        em("vector", lambda en: en.tensor_mul(u2s[:, :], vt[:, 0:NT], sc1b[:, :]))
                        em("vector", lambda en: en.tensor_scalar(u2s[:, :], u2s[:, :], cm_col, cm_col, ALU.add, ALU.subtract))
                        em("vector", lambda en, c=ct: en.tensor_scalar(
                            h2q[:, NT * c:NT * (c + 1)], u2s[:, :], 127.0, None, ALU.min))
                        mark("vector", "V")
                if p == 0:
                    V_mx = cnt["V"]
                    wait("tensor", "V", V_mx)
                    for j in range(8):
                        em("tensor", lambda en, jj=j: en.transpose(
                            pb[6].bitcast(BF16)[:, 128 * (jj % 4):128 * (jj % 4) + 128],
                            mxa[:, 128 * jj:128 * (jj + 1)], identb))
                        Tt = mark("tensor", "T")
                        wait("vector", "T", Tt)
                        em("vector", lambda en, jj=j: en.reduce_max(
                            t0c[:, jj:jj + 1], pb[6].bitcast(BF16)[:, 128 * (jj % 4):128 * (jj % 4) + 128], AX.X))
                        Vt = mark("vector", "V")
                        wait("tensor", "V", Vt)
                    for j in range(8):
                        em("tensor", lambda en, jj=j: en.transpose(
                            pb[4][0:1, 128 * (jj % 4):128 * (jj % 4) + 128],
                            t0c[:, jj:jj + 1], identf))
                        Tt = mark("tensor", "T")
                        wait("vector", "T", Tt)
                        em("vector", lambda en, jj=j: en.tensor_copy(
                            r_a[:, 128 * jj:128 * (jj + 1)], pb[4][0:1, 128 * (jj % 4):128 * (jj % 4) + 128]))
                        Vt = mark("vector", "V")
                        wait("tensor", "V", Vt)
                    wait("vector", "V", cnt["V"])
                    vchain([lambda en: en.tensor_scalar(r_a, r_a, EPS, None, ALU.max),
                            lambda en: en.reciprocal(r_b, r_a),
                            lambda en: en.tensor_scalar(r_b, r_b, 128.0, None, ALU.mult)])
                    em("vector", lambda en: en.tensor_scalar(r_a, r_a, 1.0 / 128.0, None, ALU.mult))
                    V_s1 = mark("vector", "V")
                    bcast(r_b, sc1b, V_s1)
            V_h2q = cnt["V"]

            if _DEBUG:
                wait("gpsimd", "V", V_h2q)
                em("gpsimd", lambda en: en.dma_start(dbg["dbg_h2q"][:, :], h2q[:, :]).then_inc(sD, 16))
                bump("D", 16)

            u2row = u2s[0:1, 0:1024]
            wait("tensor", "V", V_h2q)
            for j in range(8):
                em("tensor", lambda en, jj=j: en.transpose(
                    pb[4][0:1, 128 * (jj % 4):128 * (jj % 4) + 128], t0c[:, jj:jj + 1], identf))
                Tt = mark("tensor", "T")
                wait("vector", "T", Tt)
                em("vector", lambda en, jj=j: en.tensor_copy(
                    u2row[:, 128 * jj:128 * (jj + 1)], pb[4][0:1, 128 * (jj % 4):128 * (jj % 4) + 128]))
                Vt = mark("vector", "V")
                wait("tensor", "V", Vt)
            wait("vector", "V", cnt["V"])
            vchain([lambda en: en.tensor_scalar(u2row, u2row, EPS, None, ALU.max)])
            em("vector", lambda en: en.tensor_scalar(u2row, u2row, 1.0 / 128.0, None, ALU.mult))
            V_i1r = mark("vector", "V")
            V_i1 = bcast(u2row, bcA, V_i1r)
            wait("scalar", "V", V_i1)
            em("scalar", lambda en: en.activation(bcB[:, :], bcA[:, :], AF.Identity, scale=mcols[:, 0:1]))
            S_i1g = mark("scalar", "S")

            ckpt(6)
            q_T = {-2: 0, -1: 0}
            blkid = 0
            for it in range(NG):
                for wsel, dram, scol_ap, pb_pair in (("g", d_wgT, mcols[:, 4:5], (0, 1)),
                                                     ("u", d_wuT, mcols[:, 5:6], (2, 3))):
                    for half in range(2):
                        slot = blkid % 2
                        wait("sync", "T", q_T[blkid - 2])
                        em("sync", lambda en, s=slot, d=dram, i=it, h=half, sm=sW[slot]: en.dma_start(
                            wf[:, 1024 * s:1024 * (s + 1)],
                            bass.AP(d, (8 * h) * 128 * NI + i * 128, [[NI, 128], [128 * NI, 8], [1, 128]])).then_inc(sm, 16))
                        Lw = bump(f"W{slot}", 16)
                        wait("scalar", f"W{slot}", Lw)
                        if blkid == 0:
                            wait("scalar", "V", V_h2q)
                            wait("scalar", "S", S_i1g)
                        em("scalar", lambda en, s=slot, sc=scol_ap: en.activation(
                            wf[:, 1024 * s:1024 * (s + 1)], wf[:, 1024 * s:1024 * (s + 1)],
                            AF.Identity, bias=cm_col, scale=sc))
                        Sq = mark("scalar", "S")
                        wait("vector", "S", Sq)
                        em("vector", lambda en, s=slot: en.tensor_scalar(
                            wf[:, 1024 * s:1024 * (s + 1)], wf[:, 1024 * s:1024 * (s + 1)],
                            cm_col, 1.0, ALU.subtract, ALU.min))
                        em("vector", lambda en, s=slot: en.tensor_scalar(
                            wq[:, 2048 * s:2048 * s + 1024], wf[:, 1024 * s:1024 * (s + 1)],
                            -1.0, None, ALU.max))
                        Vq = mark("vector", "V")
                        wait("tensor", "V", Vq)
                        for ch in range(2):
                            for cti in range(8):
                                ct = 8 * half + cti
                                em("tensor", lambda en, s=slot, c=ct, ci=cti, cc=ch, pp=pb_pair,
                                   st=(half == 0 and cti == 0),
                                   sp=(half == 1 and cti == 7): en.matmul(
                                    pb[pp[cc]][:, 0:512],
                                    wq[:, 2048 * s + 128 * ci:2048 * s + 128 * (ci + 1)],
                                    h2q[:, NT * c + 512 * cc:NT * c + 512 * (cc + 1)],
                                    start=st, stop=sp))
                        q_T[blkid] = mark("tensor", "T")
                        blkid += 1
                wait("vector", "T", q_T[blkid - 1])
                if it == 0:
                    wait("vector", "S", S_i1g)
                for ch in range(2):
                    em("vector", lambda en, cc=ch: en.tensor_mul(
                        u2s[:, 512 * cc:512 * (cc + 1)], pb[cc][:, 0:512], bcB[:, 512 * cc:512 * (cc + 1)]))
                V_g = mark("vector", "V")
                wait("scalar", "V", V_g)
                em("scalar", lambda en: en.activation(vt[:, 0:NT], u2s[:, :], AF.Silu))
                em("scalar", lambda en: en.activation(vt[:, NT:NT + 512], pb[2][:, 0:512], AF.Identity))
                em("scalar", lambda en: en.activation(vt[:, NT + 512:2 * NT], pb[3][:, 0:512], AF.Identity))
                S_z = mark("scalar", "S")
                wait("vector", "S", S_z)
                em("vector", lambda en, i=it: en.tensor_mul(
                    zbuf[:, NT * i:NT * (i + 1)], vt[:, 0:NT], vt[:, NT:2 * NT]))
                V_z = mark("vector", "V")
                wait("tensor", "V", V_z)
                wait("tensor", "S", S_z)
            V_zdone = cnt["V"]

            if _DEBUG:
                wait("gpsimd", "V", V_zdone)
                em("gpsimd", lambda en: en.dma_start(dbg["dbg_z"][:, :], zbuf[:, 0:16 * NT]).then_inc(sD, 16))
                bump("D", 16)

            ckpt(7)
            for g in range(NG):
                for ch in range(2):
                    em("tensor", lambda en, gg=g, cc=ch: en.matmul(
                        pb[cc][:, 0:512], h128b,
                        zbuf[:, NT * gg + 512 * cc:NT * gg + 512 * (cc + 1)],
                        start=True, stop=True))
                Tg = mark("tensor", "T")
                if g % 2 == 0:
                    wait("vector", "T", Tg)
                    em("vector", lambda en, gg=g: en.tensor_copy(zbuf[:, NT * gg:NT * gg + 512], pb[0][:, 0:512]))
                    em("vector", lambda en, gg=g: en.tensor_copy(zbuf[:, NT * gg + 512:NT * (gg + 1)], pb[1][:, 0:512]))
                    Ve = mark("vector", "V")
                    wait("tensor", "V", Ve)
                else:
                    wait("scalar", "T", Tg)
                    em("scalar", lambda en, gg=g: en.activation(zbuf[:, NT * gg:NT * gg + 512], pb[0][:, 0:512], AF.Identity))
                    em("scalar", lambda en, gg=g: en.activation(zbuf[:, NT * gg + 512:NT * (gg + 1)], pb[1][:, 0:512], AF.Identity))
                    Se = mark("scalar", "S")
                    wait("tensor", "S", Se)

            NGP = NT // 8
            for gp in range(NGP):
                for q in range(4):
                    tb = gp * 4 + q
                    em("tensor", lambda en, qq=q, tt=2 * tb: en.matmul(
                        pb[6][0:64, 128 * qq:128 * (qq + 1)],
                        bass.AP(zbuf, tt, [[NG * NT, 128], [NT, NG]]),
                        identb,
                        start=(qq == 0), stop=(qq == 3)))
                    em("tensor", lambda en, qq=q, tt=2 * tb + 1: en.matmul(
                        pb[3][64:128, 128 * qq:128 * (qq + 1)],
                        bass.AP(zbuf, tt, [[NG * NT, 128], [NT, NG]]),
                        identb,
                        start=(qq == 0), stop=(qq == 3),
                        tile_position=(0, 64)))
                T2 = mark("tensor", "T")
                slot = gp % 2
                if gp % 2 == 0:
                    wait("vector", "T", T2)
                    em("vector", lambda en, s=slot: en.tensor_copy(w2b[0:64, 512 * s:512 * (s + 1)], pb[6][0:64, 0:512]))
                    em("vector", lambda en, s=slot: en.tensor_copy(w2b[64:128, 512 * s:512 * (s + 1)], pb[3][64:128, 0:512]))
                    E2 = mark("vector", "V"); E2k = "V"
                else:
                    wait("scalar", "T", T2)
                    em("scalar", lambda en, s=slot: en.activation(w2b[0:64, 512 * s:512 * (s + 1)], pb[6][0:64, 0:512], AF.Identity))
                    em("scalar", lambda en, s=slot: en.activation(w2b[64:128, 512 * s:512 * (s + 1)], pb[3][64:128, 0:512], AF.Identity))
                    E2 = mark("scalar", "S"); E2k = "S"
                wait("tensor", E2k, E2)
                for q in range(4):
                    em("tensor", lambda en, qq=q, s=slot: en.matmul(
                        pb[7][:, 128 * qq:128 * (qq + 1)],
                        w2b[:, 512 * s + 128 * qq:512 * s + 128 * (qq + 1)],
                        bmatb,
                        start=(qq == 0), stop=(qq == 3)))
                T3 = mark("tensor", "T")
                dst = bass.AP(zbuf, 8 * gp, [[NG * NT, 128], [2, 4], [1, 2], [NT, NG]])
                src = bass.AP(pb[7], 0, [[512, 128], [128, 4], [64, 2], [1, 64]])
                if gp % 2 == 0:
                    wait("scalar", "T", T3)
                    em("scalar", lambda en, d=dst, s=src: en.activation(d, s, AF.Identity))
                    E3 = mark("scalar", "S"); E3k = "S"
                else:
                    wait("vector", "T", T3)
                    em("vector", lambda en, d=dst, s=src: en.tensor_copy(d, s))
                    E3 = mark("vector", "V"); E3k = "V"
                wait("tensor", E3k, E3)
            V_fw = cnt["V"]
            S_fw = cnt["S"]

            ckpt(8)
            wait("vector", "S", S_fw)
            mnn = vt[:, 0:NT]
            em("vector", lambda en: en.tensor_copy(mxa[:, :], zbuf[:, 0:NT]))
            em("vector", lambda en: en.tensor_copy(mnn, zbuf[:, 0:NT]))
            for g in range(1, NG):
                em("vector", lambda en, gg=g: en.tensor_tensor(
                    mxa[:, :], mxa[:, :], zbuf[:, NT * gg:NT * (gg + 1)], ALU.max))
                em("vector", lambda en, gg=g: en.tensor_tensor(
                    mnn, mnn, zbuf[:, NT * gg:NT * (gg + 1)], ALU.min))
            em("vector", lambda en: en.scalar_tensor_tensor(mxa[:, :], mnn, -1.0, mxa[:, :], ALU.mult, ALU.max))
            V_mx2 = mark("vector", "V")
            wait("tensor", "V", V_mx2)
            for j in range(8):
                em("tensor", lambda en, jj=j: en.transpose(
                    pb[6].bitcast(BF16)[:, 128 * (jj % 4):128 * (jj % 4) + 128], mxa[:, 128 * jj:128 * (jj + 1)], identb))
                Tt = mark("tensor", "T")
                wait("vector", "T", Tt)
                em("vector", lambda en, jj=j: en.reduce_max(
                    t0c[:, jj:jj + 1], pb[6].bitcast(BF16)[:, 128 * (jj % 4):128 * (jj % 4) + 128], AX.X))
                Vt = mark("vector", "V")
                wait("tensor", "V", Vt)
            for j in range(8):
                em("tensor", lambda en, jj=j: en.transpose(
                    pb[4][0:1, 128 * (jj % 4):128 * (jj % 4) + 128], t0c[:, jj:jj + 1], identf))
                Tt = mark("tensor", "T")
                wait("vector", "T", Tt)
                em("vector", lambda en, jj=j: en.tensor_copy(
                    r_a[:, 128 * jj:128 * (jj + 1)], pb[4][0:1, 128 * (jj % 4):128 * (jj % 4) + 128]))
                Vt = mark("vector", "V")
                wait("tensor", "V", Vt)
            wait("vector", "V", cnt["V"])
            vchain([lambda en: en.tensor_mul(r_b, r_a, bcA[0:1, 0:NT]),
                    lambda en: en.tensor_scalar(r_b, r_b, mcols[0:1, 1:2], CNORM, ALU.mult, ALU.mult),
                    lambda en: en.tensor_scalar(r_b, r_b, EPS, None, ALU.max),
                    lambda en: en.tensor_scalar(r_m, r_b, 1.0 / 128.0, None, ALU.mult),
                    lambda en: en.reciprocal(r_a, r_b),
                    lambda en: en.tensor_mul(r_a, r_a, bcA[0:1, 0:NT])])
            em("vector", lambda en: en.tensor_scalar(r_a, r_a, mcols[0:1, 1:2], 128.0 * CNORM, ALU.mult, ALU.mult))
            V_q2r = mark("vector", "V")
            QBb = wq[:, 0:1024]
            IBb = bcA
            wait("tensor", "V", V_q2r)
            last = None
            for ch in range(2):
                em("tensor", lambda en, c=ch: en.matmul(
                    pb[4][:, 0:512], orow[:, :], r_a[:, 512 * c:512 * (c + 1)], start=True, stop=True))
                Tb = mark("tensor", "T")
                wait("vector", "T", Tb)
                em("vector", lambda en, c=ch: en.tensor_copy(QBb[:, 512 * c:512 * (c + 1)], pb[4][:, 0:512]))
                Vb = mark("vector", "V")
                wait("tensor", "V", Vb)
                em("tensor", lambda en, c=ch: en.matmul(
                    pb[4][:, 0:512], orow[:, :], r_m[:, 512 * c:512 * (c + 1)], start=True, stop=True))
                Tb2 = mark("tensor", "T")
                wait("vector", "T", Tb2)
                em("vector", lambda en, c=ch: en.tensor_copy(IBb[:, 512 * c:512 * (c + 1)], pb[4][:, 0:512]))
                last = mark("vector", "V")
                wait("tensor", "V", last)
            for g in range(NG):
                em("vector", lambda en, gg=g: en.tensor_mul(u2s[:, :], zbuf[:, NT * gg:NT * (gg + 1)], QBb[:, :]))
                Vm = mark("vector", "V")
                wait("scalar", "V", Vm)
                em("scalar", lambda en: en.activation(u2s[:, :], u2s[:, :], AF.Identity, bias=cm_col))
                Sa = mark("scalar", "S")
                wait("vector", "S", Sa)
                em("vector", lambda en, gg=g: en.tensor_scalar(
                    zbuf[:, NT * gg:NT * (gg + 1)], u2s[:, :], cm_col, 127.0, ALU.subtract, ALU.min))
                mark("vector", "V")
            V_xq = cnt["V"]

            if _DEBUG:
                wait("gpsimd", "V", V_xq)
                em("gpsimd", lambda en: en.dma_start(dbg["dbg_xq"][:, :], zbuf[:, 0:16 * NT]).then_inc(sD, 16))
                bump("D", 16)

            ckpt(9)
            d_T = {-2: 0, -1: 0}
            dblk = 0
            xo_V = {-2: 0, -1: 0}
            for ot in range(NCT):
                wait("sync", "V", xo_V[ot - 2])
                if ot == 0:
                    wait("sync", "V", V_xq)
                em("sync", lambda en, i=ot % 2, o=ot, sm=sX[ot % 2]: en.dma_start(
                    xcol[:, NT * i:NT * (i + 1)],
                    bass.AP(d_x2, o * 128 * NT, [[NT, 128], [1, NT]])).then_inc(sm, 16))
                Lx2 = bump(f"X{ot % 2}", 16)
                Td = None
                for sub in range(8):
                    slot = dblk % 2
                    wait("sync", "T", d_T[dblk - 2])
                    em("sync", lambda en, s=slot, o=ot, sb_=sub, sm=sW[slot]: en.dma_start(
                        wf[:, 1024 * s:1024 * (s + 1)],
                        bass.AP(d_wdT, (8 * sb_) * 128 * C + o * 128, [[C, 128], [128 * C, 8], [1, 128]])).then_inc(sm, 16))
                    Lw = bump(f"W{slot}", 16)
                    wait("scalar", f"W{slot}", Lw)
                    if dblk == 0:
                        wait("scalar", "V", V_xq)
                    em("scalar", lambda en, s=slot: en.activation(
                        wf[:, 1024 * s:1024 * (s + 1)], wf[:, 1024 * s:1024 * (s + 1)],
                        AF.Identity, bias=cm_col, scale=mcols[:, 6:7]))
                    Sq = mark("scalar", "S")
                    wait("vector", "S", Sq)
                    em("vector", lambda en, s=slot: en.tensor_scalar(
                        wf[:, 1024 * s:1024 * (s + 1)], wf[:, 1024 * s:1024 * (s + 1)],
                        cm_col, 1.0, ALU.subtract, ALU.min))
                    em("vector", lambda en, s=slot: en.tensor_scalar(
                        wq[:, 2048 + 1024 * s:2048 + 1024 * (s + 1)], wf[:, 1024 * s:1024 * (s + 1)],
                        -1.0, None, ALU.max))
                    Vq = mark("vector", "V")
                    wait("tensor", "V", Vq)
                    for ch in range(2):
                        for gg in range(8):
                            em("tensor", lambda en, s=slot, g_=8 * sub + gg, gi=gg, cc=ch,
                               st=(sub == 0 and gg == 0),
                               sp=(sub == 7 and gg == 7): en.matmul(
                                pb[cc][:, 0:512],
                                wq[:, 2048 + 1024 * s + 128 * gi:2048 + 1024 * s + 128 * (gi + 1)],
                                zbuf[:, NT * g_ + 512 * cc:NT * g_ + 512 * (cc + 1)],
                                start=st, stop=sp))
                    Td = mark("tensor", "T")
                    d_T[dblk] = Td
                    dblk += 1
                wait("vector", "T", Td)
                wait("vector", f"X{ot % 2}", Lx2)
                if ot == 0:
                    wait("vector", "S", S_gbd)
                for ch in range(2):
                    em("vector", lambda en, cc=ch: en.tensor_mul(
                        u2s[:, 512 * cc:512 * (cc + 1)], pb[cc][:, 0:512], IBb[:, 512 * cc:512 * (cc + 1)]))
                for ch in range(2):
                    em("vector", lambda en, i=ot % 2, cc=ch, o=ot: en.scalar_tensor_tensor(
                        xcol[:, NT * i + 512 * cc:NT * i + 512 * (cc + 1)],
                        u2s[:, 512 * cc:512 * (cc + 1)], GBD[:, o:o + 1],
                        xcol[:, NT * i + 512 * cc:NT * i + 512 * (cc + 1)],
                        ALU.mult, ALU.add))
                V_out = mark("vector", "V")
                xo_V[ot] = V_out
                wait("tensor", "V", V_out)
                wait("sync", "V", V_out)
                em("sync", lambda en, i=ot % 2, o=ot: en.dma_start(
                    bass.AP(d_out, o * 128 * NT, [[NT, 128], [1, NT]]),
                    xcol[:, NT * i:NT * (i + 1)]).then_inc(sO, 16))
                bump("O", 16)
            O_final = cnt["O"]
        except _Stop:
            pass
        if cnt["O"]:
            wait("gpsimd", "O", cnt["O"])
        if _DEBUG2:
            for nm, src in [("dbg2_kc", kc[:, :]), ("dbg2_pcol", pcol[:, :]),
                            ("dbg2_mcols", mcols[:, :]), ("dbg2_srow", srow[:, :])]:
                em("gpsimd", lambda en, n_=nm, s_=src: en.dma_start(
                    dbg2[n_][:, :], s_).then_inc(sD, 16))
                bump("D", 16)
            wait("gpsimd", "D", cnt["D"])
        if _DEBUG:
            wait("gpsimd", "D", cnt["D"]) if cnt["D"] else None

        @block.sync
        def _(eng):
            for f in sched["sync"]:
                f(eng)

        @block.gpsimd
        def _(eng):
            for f in sched["gpsimd"]:
                f(eng)

        @block.tensor
        def _(eng):
            for f in sched["tensor"]:
                f(eng)

        @block.vector
        def _(eng):
            for f in sched["vector"]:
                f(eng)

        @block.scalar
        def _(eng):
            for f in sched["scalar"]:
                f(eng)

    return nc


def _hadamard(n):
    H = np.array([[1.0]])
    while H.shape[0] < n:
        H = np.block([[H, H], [H, -H]])
    return H


def _colsof(v):
    return np.ascontiguousarray(np.asarray(v, np.float32).reshape(16, 128).T)


def make_inmaps(inputs):
    f32 = np.float32
    x = np.asarray(inputs["L0_attn_x"], f32)
    wvT = np.ascontiguousarray(np.asarray(inputs["L0_attn_w_value"], f32).T)
    wrT = np.ascontiguousarray(np.asarray(inputs["L0_attn_w_rec"], f32).T)
    wkT = np.ascontiguousarray(np.asarray(inputs["L0_attn_w_key"], f32).T)
    woT = np.ascontiguousarray(np.asarray(inputs["L0_attn_w_out"], f32).T)
    wgT = np.ascontiguousarray(np.asarray(inputs["L0_ffn_w_gate"], f32).T)
    wuT = np.ascontiguousarray(np.asarray(inputs["L0_ffn_w_up"], f32).T)
    wdT = np.ascontiguousarray(np.asarray(inputs["L0_ffn_w_down"], f32).T)
    wmod = np.asarray(inputs["L0_wmod"], f32)
    cT = np.ascontiguousarray(np.asarray(inputs["L0_c"], f32).T)

    H128 = _hadamard(128)
    H64 = _hadamard(64)
    identb = np.eye(128)
    Bmat = np.zeros((128, 128))
    for d_ in range(2):
        Bmat[d_ * 64:(d_ + 1) * 64, d_ * 64:(d_ + 1) * 64] = H64
    onescol = np.ones((128, 1))
    constb = np.concatenate([identb, H128, Bmat, onescol], axis=1).astype(ml_dtypes.bfloat16)
    constf = np.eye(128, dtype=f32)
    onesrow = np.ones((1, 128), f32)

    tf = _colsof(inputs["L0_attn_time_first"])
    tmk = _colsof(np.asarray(inputs["L0_attn_time_mix_k"]).ravel())
    tmv = _colsof(np.asarray(inputs["L0_attn_time_mix_v"]).ravel())
    tmr = _colsof(np.asarray(inputs["L0_attn_time_mix_r"]).ravel())
    xxc = _colsof(np.asarray(inputs["L0_attn_xx"]).ravel())
    aac = _colsof(np.asarray(inputs["L0_attn_aa"]).ravel())
    bbc = _colsof(np.asarray(inputs["L0_attn_bb"]).ravel())
    ppc = _colsof(np.asarray(inputs["L0_attn_pp"]).ravel())
    zc = np.zeros_like(aac)
    ppn = np.full_like(ppc, -1e38)

    in_maps = []
    for i in range(8):
        b, h = i // 2, i % 2
        xT = np.ascontiguousarray(x[b, h * NT:(h + 1) * NT, :].T)
        pvv = np.concatenate(
            [tf, tmk, tmv, tmr, xxc] + ([aac, bbc, ppc] if h == 0 else [zc, zc, ppn]), axis=1)
        mask = np.zeros((128, NMOD * 4), f32)
        mask[:, b::4] = 1.0
        in_maps.append({
            "xT": xT, "wvT": wvT, "wrT": wrT, "wkT": wkT, "woT": woT,
            "wgT": wgT, "wuT": wuT, "wdT": wdT,
            "wmod_sl": np.ascontiguousarray(wmod[i * NWM:(i + 1) * NWM, :]),
            "cT_sl": np.ascontiguousarray(cT[i * NWM:(i + 1) * NWM, :]),
            "pvecs": np.ascontiguousarray(pvv),
            "constb": constb, "constf": constf, "onesrow": onesrow,
            "maskf": mask,
            "wms_g": np.ascontiguousarray(wgT[:, i * 1024:(i + 1) * 1024]),
            "wms_u": np.ascontiguousarray(wuT[:, i * 1024:(i + 1) * 1024]),
            "wms_d": np.ascontiguousarray(wdT[:, i * 256:(i + 1) * 256]),
        })
    return in_maps


def kernel(**inputs):
    in_maps = make_inmaps(inputs)
    nc = _build()
    res = run_bass_kernel_spmd(nc, in_maps, core_ids=list(range(8)))
    out = np.empty((B, T, C), np.float32)
    for i in range(8):
        b, h = i // 2, i % 2
        out[b, h * NT:(h + 1) * NT, :] = res.results[i]["outT"].T
    return out


# version 04 (final)
# speedup vs baseline: 1.0340x; 1.0340x over previous
import numpy as np
import ml_dtypes
import concourse.bass as bass
import concourse.mybir as mybir
from concourse.bass_utils import run_bass_kernel_spmd
from contextlib import ExitStack

F32 = mybir.dt.float32
BF16 = mybir.dt.bfloat16
ALU = mybir.AluOpType
AF = mybir.ActivationFunctionType
AX = mybir.AxisListType

B, T, C, NI = 4, 2048, 2048, 8192
NT = 1024
NCT = C // 128
NG = NI // 128
NMOD = 6 * C // 128
EPS = 1e-5
C23 = 12582912.0
KFLAG = float((1.5 * 2 ** 52 - C23) * 2 ** 30)
NWM = C // 8
CNORM = float(np.float32(1.0 / np.sqrt(8192.0)))

_DEBUG = False
_DEBUG2 = False
_UPTO = None


class _Stop(Exception):
    pass


def _build():
    nc = bass.Bass("TRN2", target_bir_lowering=False, debug=False)

    d_xT = nc.dram_tensor("xT", [C, NT], F32, kind="ExternalInput")
    d_wvT = nc.dram_tensor("wvT", [C, C], F32, kind="ExternalInput")
    d_wrT = nc.dram_tensor("wrT", [C, C], F32, kind="ExternalInput")
    d_wkT = nc.dram_tensor("wkT", [C, C], F32, kind="ExternalInput")
    d_woT = nc.dram_tensor("woT", [C, C], F32, kind="ExternalInput")
    d_wgT = nc.dram_tensor("wgT", [NG * 2 * 128, 1024], F32, kind="ExternalInput")
    d_wuT = nc.dram_tensor("wuT", [NG * 2 * 128, 1024], F32, kind="ExternalInput")
    d_wdT = nc.dram_tensor("wdT", [NCT * 8 * 128, 1024], F32, kind="ExternalInput")
    d_wmod = nc.dram_tensor("wmod_sl", [NWM, 6 * C], F32, kind="ExternalInput")
    d_cT = nc.dram_tensor("cT_sl", [NWM, B], F32, kind="ExternalInput")
    d_pv = nc.dram_tensor("pvecs", [128, 128], F32, kind="ExternalInput")
    d_cb = nc.dram_tensor("constb", [128, 385], BF16, kind="ExternalInput")
    d_cf = nc.dram_tensor("constf", [128, 128], F32, kind="ExternalInput")
    d_or = nc.dram_tensor("onesrow", [1, 128], F32, kind="ExternalInput")
    d_mask = nc.dram_tensor("maskf", [128, NMOD * 4], F32, kind="ExternalInput")
    d_msg = nc.dram_tensor("wms_g", [C, 1024], F32, kind="ExternalInput")
    d_msu = nc.dram_tensor("wms_u", [C, 1024], F32, kind="ExternalInput")
    d_msd = nc.dram_tensor("wms_d", [NI, 256], F32, kind="ExternalInput")
    d_out = nc.dram_tensor("outT", [C, NT], F32, kind="ExternalOutput")

    d_x2 = nc.dram_tensor("x2_scratch", [C, NT], F32, kind="Internal")
    d_ccmi = nc.dram_tensor("cc_mod_in", [128, NMOD * 4 + 4], F32, kind="Internal")
    d_ccmo = nc.dram_tensor("cc_mod_out", [128, NMOD * 4 + 4], F32, kind="Internal")

    dbg2 = {}
    if _DEBUG2:
        for nm, shp in [("dbg2_kc", [128, 96]), ("dbg2_pcol", [128, 176]),
                        ("dbg2_mcols", [128, 8]), ("dbg2_srow", [1, 8])]:
            dbg2[nm] = nc.dram_tensor(nm, shp, F32, kind="ExternalOutput")

    dbg = {}
    if _DEBUG:
        for nm, shp in [("dbg_mod", [128, 96]), ("dbg_xv", [128, NCT * NT]),
                        ("dbg_m", [128, NCT * NT]), ("dbg_h2q", [128, NCT * NT]),
                        ("dbg_z", [128, 16 * NT]), ("dbg_xq", [128, 16 * NT]),
                        ("dbg_cols", [128, 192]), ("dbg_rows", [1, 4096])]:
            dbg[nm] = nc.dram_tensor(nm, shp, F32, kind="ExternalOutput")

    with ExitStack() as stk:
        e = stk.enter_context
        zbuf = e(nc.sbuf_tensor("zbuf", [128, NG * NT], BF16))
        h2q = e(nc.sbuf_tensor("h2q", [128, NCT * NT], BF16))
        u2s = e(nc.sbuf_tensor("u2s", [128, NT], F32))
        xcol = e(nc.sbuf_tensor("xcol", [128, 2 * NT], F32))
        bcA = e(nc.sbuf_tensor("bcA", [128, NT], F32))
        bcB = e(nc.sbuf_tensor("bcB", [128, NT], F32))
        wf = e(nc.sbuf_tensor("wf", [128, 2 * 1024], F32))
        wq = e(nc.sbuf_tensor("wq", [128, 2 * 2048], BF16))
        vt = e(nc.sbuf_tensor("vt", [128, 2 * NT], BF16))
        pv = e(nc.sbuf_tensor("pv", [128, 128], F32))
        cstb = e(nc.sbuf_tensor("cstb", [128, 385], BF16))
        cstf = e(nc.sbuf_tensor("cstf", [128, 128], F32))
        orow = e(nc.sbuf_tensor("orow", [1, 128], F32))
        pcol = e(nc.sbuf_tensor("pcol", [128, 176], F32))
        scol = e(nc.sbuf_tensor("scol", [128, 4], F32))
        mcols = e(nc.sbuf_tensor("mcols", [128, 8], F32))
        kc = e(nc.sbuf_tensor("kc", [128, 96], F32))
        xk0b = e(nc.sbuf_tensor("xk0b", [128, 16], BF16))
        msel = e(nc.sbuf_tensor("msel", [128, 96], F32))
        srow = e(nc.sbuf_tensor("srow", [1, 8], F32))
        csb = e(nc.sbuf_tensor("csb", [128, 8], F32))
        silc = e(nc.sbuf_tensor("silc", [128, 8], BF16))
        pb = [e(nc.psum_tensor(f"pb{i}", [128, 512], F32)) for i in range(8)]
        armod = xcol[:, 0:384]
        modst = xcol[:, 384:768]
        mxa = h2q[:, 1536:2560]
        w2b = h2q[:, 2560:3584]
        xbs = h2q[:, 4096:6144]
        sc1b = wq[:, 0:1024]
        rows = h2q[:, 6144:12288].bitcast(F32)[0:1, :]
        sHD = e(nc.semaphore("sHD"))
        sGH = e(nc.semaphore("sGH"))
        sCC = e(nc.semaphore("sCC"))
        sB = [e(nc.semaphore("sB0")), e(nc.semaphore("sB1"))]
        sA = [e(nc.semaphore("sA0")), e(nc.semaphore("sA1"))]
        sW = [e(nc.semaphore("sW0")), e(nc.semaphore("sW1"))]
        sX = [e(nc.semaphore("sX0")), e(nc.semaphore("sX1"))]
        sD = e(nc.semaphore("sD"))
        sT = e(nc.semaphore("sT"))
        sV = e(nc.semaphore("sV"))
        sS = e(nc.semaphore("sS"))
        sO = e(nc.semaphore("sO"))
        sC = e(nc.semaphore("sC"))
        block = e(nc.Block())

        sched = {k: [] for k in ("sync", "gpsimd", "tensor", "vector", "scalar")}
        cnt = {"HD": 0, "GH": 0, "CC": 0, "B0": 0, "B1": 0, "A0": 0, "A1": 0,
               "W0": 0, "W1": 0, "X0": 0, "X1": 0, "D": 0,
               "T": 0, "V": 0, "S": 0, "O": 0, "C": 0}
        SEM = {"HD": sHD, "GH": sGH, "CC": sCC, "B0": sB[0], "B1": sB[1],
               "A0": sA[0], "A1": sA[1], "W0": sW[0], "W1": sW[1],
               "X0": sX[0], "X1": sX[1], "D": sD,
               "T": sT, "V": sV, "S": sS, "O": sO, "C": sC}

        def em(eng, f):
            sched[eng].append(f)

        def wait(eng, key, val):
            if val and val > 0:
                em(eng, lambda en, s=SEM[key], v=val: en.wait_ge(s, v))

        def bump(key, k=1):
            cnt[key] += k
            return cnt[key]

        def flush(eng, key):
            v = mark(eng, key)
            wait(eng, key, v)

        def vchain(ops):
            for f in ops:
                em("vector", f)
                flush("vector", "V")

        def mark(eng, key):
            last = sched[eng].pop()

            def wrapped(en, f=last, sm=SEM[key]):
                ins = f(en)
                ins.then_inc(sm, 1)
            sched[eng].append(wrapped)
            return bump(key)

        def ckpt(k):
            if _UPTO is not None and _UPTO <= k:
                raise _Stop()

        XV0 = 0
        M0 = 20 * NT
        XR0 = 40 * NT

        identb = cstb[:, 0:128]
        h128b = cstb[:, 128:256]
        bmatb = cstb[:, 256:384]
        onescol_b = cstb[:, 384:385]
        identf = cstf[:, 0:128]

        tfc, tmkc, tmvc, tmrc, xxc, aac, bbc, ppc = [pv[:, 16 * j:16 * (j + 1)] for j in range(8)]
        AV, BV, AR, BR, AK, BK, GA2, GB2, CB1, SBC, GBD = [pcol[:, 16 * j:16 * (j + 1)] for j in range(11)]
        cm_col = scol[:, 0:1]
        fl_col = scol[:, 1:2]
        ukcol = kc[:, 0:16]
        kcolv = kc[:, 32:48]
        v0c = kc[:, 48:64]
        r0c = kc[:, 64:80]
        t0c = kc[:, 80:96]
        r_m = rows[:, 0:1024]
        r_a = rows[:, 1024:2048]
        r_b = rows[:, 2048:3072]
        r_rstd = r_b

        try:
            em("sync", lambda en: en.dma_start(cstb[:, :], d_cb[:, :]).then_inc(sHD, 16)); bump("HD", 16)
            em("sync", lambda en: en.dma_start(cstf[:, :], d_cf[:, :]).then_inc(sHD, 16)); bump("HD", 16)
            em("sync", lambda en: en.dma_start(orow[:, :], d_or[:, :]).then_inc(sHD, 16)); bump("HD", 16)
            em("sync", lambda en: en.dma_start(pv[:, :], d_pv[:, :]).then_inc(sHD, 16)); bump("HD", 16)
            em("sync", lambda en: en.dma_start(armod[:, :], d_mask[:, :]).then_inc(sHD, 16)); bump("HD", 16)
            em("sync", lambda en: en.dma_start(
                csb[:, :], bass.AP(d_cT, 0, [[B, 128], [128 * B, 2], [1, B]])).then_inc(sHD, 16))
            HD_all = bump("HD", 16)
            L_cstb = L_orow = L_pv = L_mask = L_csb = HD_all
            em("gpsimd", lambda en: en.dma_start(
                bass.AP(zbuf, 0, [[NG * NT, 128], [6 * C, 2], [1, 6 * C]]),
                bass.AP(d_wmod, 0, [[6 * C, 128], [6 * C * 128, 2], [1, 6 * C]])).then_inc(sGH, 16))
            G_wmod = bump("GH", 16)

            ckpt(0.2)
            wait("scalar", "HD", L_csb)
            em("scalar", lambda en: en.activation(silc[:, :], csb[:, :], AF.Silu))
            S_silc = mark("scalar", "S")

            wait("vector", "HD", L_cstb)
            vchain([lambda en: en.memset(fl_col, 2.0 ** -30),
                    lambda en: en.tensor_scalar(fl_col, fl_col, 1.0, 1.0, ALU.add, ALU.subtract)])
            em("vector", lambda en: en.tensor_scalar(cm_col, fl_col, KFLAG, C23, ALU.mult, ALU.add))
            V_cmagic = mark("vector", "V")

            ckpt(0.3)
            wait("tensor", "GH", G_wmod)
            wait("tensor", "S", S_silc)
            for ot in range(NMOD):
                for cs in range(2):
                    em("tensor", lambda en, o=ot, c=cs, st=(ot == 0 and cs == 0),
                       sp=(ot == NMOD - 1 and cs == 1): en.matmul(
                        pb[6][:, 4 * o:4 * (o + 1)],
                        zbuf[:, c * 6 * C + 128 * o: c * 6 * C + 128 * o + 128],
                        silc[:, 4 * c:4 * (c + 1)],
                        start=st, stop=sp))
            T_mod = mark("tensor", "T")
            ckpt(0.4)
            wait("vector", "T", T_mod)
            em("vector", lambda en: en.tensor_copy(modst[:, :], pb[6][:, 0:NMOD * 4]))
            V_modev = mark("vector", "V")
            ckpt(0.5)
            mean_parts = [(d_msg, 0), (d_msu, 32), (d_msd, 64)]
            V_mb = {}
            mbi = 0
            for wdram, prow in mean_parts:
                nblk = 16
                for blk in range(nblk):
                    if mbi >= 2:
                        wait("sync", "T", V_mb[mbi - 2])
                    if wdram is d_msd:
                        srcap = bass.AP(wdram, blk * 4 * 128 * 256, [[256, 128], [128 * 256, 4], [1, 256]])
                    else:
                        srcap = bass.AP(wdram, blk * 128 * 1024, [[1024, 128], [1, 1024]])
                    em("sync", lambda en, sap=srcap, i=mbi % 2, sm=sW[mbi % 2]: en.dma_start(
                        wf[:, 1024 * i:1024 * (i + 1)], sap).then_inc(sm, 16))
                    Lb = bump(f"W{mbi % 2}", 16)
                    wait("scalar", f"W{mbi % 2}", Lb)
                    em("scalar", lambda en, i=mbi % 2: en.activation(
                        wq[:, 2048 * i:2048 * i + 1024], wf[:, 1024 * i:1024 * (i + 1)], AF.Abs))
                    Vb = mark("scalar", "S")
                    wait("tensor", "S", Vb)
                    for ch in range(2):
                        em("tensor", lambda en, i=mbi % 2, c=ch, pr=prow,
                           st=(blk == 0 and ch == 0), sp=(blk == nblk - 1 and ch == 1): en.matmul(
                            pb[7][pr:pr + 1, 0:512], onescol_b,
                            wq[:, 2048 * i + 512 * c: 2048 * i + 512 * (c + 1)],
                            start=st, stop=sp))
                    Tb = mark("tensor", "T")
                    V_mb[mbi] = Tb
                    mbi += 1
            T_mean = cnt["T"]
            wait("vector", "T", T_mean)
            em("vector", lambda en: en.reduce_sum(srow[:, 0:1], pb[7][0:1, 0:512], AX.X))
            em("vector", lambda en: en.reduce_sum(srow[:, 1:2], pb[7][32:33, 0:512], AX.X))
            em("vector", lambda en: en.reduce_sum(srow[:, 2:3], pb[7][64:65, 0:512], AX.X))
            em("vector", lambda en: en.memset(srow[:, 3:4], 0.0))
            V_sums = mark("vector", "V")
            wait("gpsimd", "V", V_sums)
            em("gpsimd", lambda en: en.dma_start(d_ccmi[:, 0:NMOD * 4], modst[:, :]).then_inc(sCC, 16)); bump("CC", 16)
            em("gpsimd", lambda en: en.dma_start(
                d_ccmi[0:1, NMOD * 4:NMOD * 4 + 4], srow[:, 0:4]).then_inc(sCC, 16)); G_mi = bump("CC", 16)
            wait("gpsimd", "CC", G_mi)
            em("gpsimd", lambda en: en.collective_compute(
                "AllReduce", ALU.add, replica_groups=[[0, 1, 2, 3, 4, 5, 6, 7]],
                ins=[d_ccmi[:, :]], outs=[d_ccmo[:, :]]).then_inc(sC, 1))
            C_mod = bump("C")
            wait("gpsimd", "C", C_mod)
            em("gpsimd", lambda en: en.dma_start(modst[:, :], d_ccmo[:, 0:NMOD * 4]).then_inc(sCC, 16)); bump("CC", 16)
            em("gpsimd", lambda en: en.dma_start(
                srow[:, 4:8], d_ccmo[0:1, NMOD * 4:NMOD * 4 + 4]).then_inc(sCC, 16)); G_mo = bump("CC", 16)
            ckpt(0.8)
            wait("vector", "CC", G_mo)
            wait("vector", "HD", L_mask)
            em("vector", lambda en: en.tensor_mul(modst[:, :], modst[:, :], armod[:, :]))
            em("vector", lambda en: en.reduce_sum(
                msel[:, :], modst.rearrange("p (a b) -> p a b", b=4), AX.X))
            V_msel = mark("vector", "V")

            sa_c = msel[:, 0:16]; ca_c = msel[:, 16:32]; ga_c = msel[:, 32:48]
            sb_c = msel[:, 48:64]; cb_c = msel[:, 64:80]; gb_c = msel[:, 80:96]

            wait("vector", "HD", L_pv)
            wait("vector", "V", V_msel)

            vchain([
                lambda en: en.tensor_scalar(CB1, ca_c, 1.0, None, ALU.add),
                lambda en: en.tensor_mul(AV, CB1, tmvc),
                lambda en: en.tensor_mul(AR, CB1, tmrc),
                lambda en: en.tensor_mul(AK, CB1, tmkc),
                lambda en: en.tensor_mul(BV, sa_c, tmvc),
                lambda en: en.tensor_scalar(SBC, tmvc, -1.0, 1.0, ALU.mult, ALU.add),
                lambda en: en.tensor_mul(SBC, SBC, xxc),
                lambda en: en.tensor_add(BV, BV, SBC),
                lambda en: en.tensor_mul(BR, sa_c, tmrc),
                lambda en: en.tensor_scalar(SBC, tmrc, -1.0, 1.0, ALU.mult, ALU.add),
                lambda en: en.tensor_mul(SBC, SBC, xxc),
                lambda en: en.tensor_add(BR, BR, SBC),
                lambda en: en.tensor_mul(BK, sa_c, tmkc),
                lambda en: en.tensor_scalar(SBC, tmkc, -1.0, 1.0, ALU.mult, ALU.add),
                lambda en: en.tensor_mul(SBC, SBC, xxc),
                lambda en: en.tensor_add(BK, BK, SBC),
                lambda en: en.scalar_tensor_tensor(GA2, ga_c, -1.0, ga_c, ALU.mult, ALU.max),
                lambda en: en.tensor_scalar(GA2, GA2, 0.25, None, ALU.max),
                lambda en: en.scalar_tensor_tensor(GB2, gb_c, -1.0, gb_c, ALU.mult, ALU.max),
                lambda en: en.tensor_scalar(GB2, GB2, 0.25, None, ALU.max),
                lambda en: en.tensor_scalar(CB1, cb_c, 1.0, None, ALU.add),
                lambda en: en.tensor_copy(SBC, sb_c),
            ])
            V_dcols = cnt["V"]
            wait("scalar", "V", V_dcols)
            em("scalar", lambda en: en.activation(t0c, ga_c, AF.Sign))
            S_sga = mark("scalar", "S")
            wait("vector", "S", S_sga)
            em("vector", lambda en: en.tensor_mul(GA2, GA2, t0c))
            V_ga = mark("vector", "V")
            wait("scalar", "V", V_ga)
            em("scalar", lambda en: en.activation(t0c, gb_c, AF.Sign))
            S_sgb = mark("scalar", "S")
            wait("vector", "S", S_sgb)
            em("vector", lambda en: en.tensor_mul(GB2, GB2, t0c))
            V_cols = mark("vector", "V")

            ckpt(1)
            wait("vector", "CC", G_mo)
            vchain([lambda en: en.tensor_scalar(srow[:, 4:8], srow[:, 4:8], 1.0 / (C * NI), EPS, ALU.mult, ALU.max)])
            em("vector", lambda en: en.reciprocal(srow[:, 0:4], srow[:, 4:8]))
            V_means = mark("vector", "V")
            wait("tensor", "V", V_means)
            wait("tensor", "HD", L_orow)
            em("tensor", lambda en: en.matmul(pb[4][:, 0:4], orow[:, :], srow[:, 4:8], start=True, stop=True))
            em("tensor", lambda en: en.matmul(pb[4][:, 4:8], orow[:, :], srow[:, 0:4], start=True, stop=True))
            T_mc = mark("tensor", "T")
            wait("vector", "T", T_mc)
            em("vector", lambda en: en.tensor_copy(mcols[:, :], pb[4][:, 0:8]))
            V_mcols = mark("vector", "V")
            wait("scalar", "V", V_mcols)
            em("scalar", lambda en: en.activation(GBD, GB2, AF.Identity, scale=mcols[:, 2:3]))
            S_gbd = mark("scalar", "S")

            ckpt(2)
            xb_T = {-2: 0, -1: 0}
            for ct in range(NCT):
                wait("gpsimd", "T", xb_T[ct - 2])
                em("gpsimd", lambda en, i=ct % 2, c=ct, sm=sB[ct % 2]: en.dma_start(
                    xbs[:, NT * i:NT * (i + 1)],
                    bass.AP(d_xT, c * 128 * NT, [[NT, 128], [1, NT]])).then_inc(sm, 16))
                Gx = bump(f"B{ct % 2}", 16)
                wait("scalar", f"B{ct % 2}", Gx)
                em("scalar", lambda en, i=ct % 2: en.activation(
                    vt[:, NT * i:NT * (i + 1)], xbs[:, NT * i:NT * (i + 1)], AF.Square))
                Ss = mark("scalar", "S")
                wait("tensor", "S", Ss)
                for ch in range(2):
                    em("tensor", lambda en, i=ct % 2, c=ch, st=(ct == 0), sp=(ct == NCT - 1): en.matmul(
                        pb[5][32 * c:32 * c + 1, 0:512], onescol_b,
                        xbs[:, NT * i + 512 * c:NT * i + 512 * (c + 1)], start=st, stop=sp))
                    em("tensor", lambda en, i=ct % 2, c=ch, st=(ct == 0), sp=(ct == NCT - 1): en.matmul(
                        pb[5 if c == 0 else 2][64 * (1 - c):64 * (1 - c) + 1, 0:512], onescol_b,
                        vt[:, NT * i + 512 * c:NT * i + 512 * (c + 1)], start=st, stop=sp))
                xb_T[ct] = mark("tensor", "T")

            def ln_rows():
                vchain([
                    lambda en: en.tensor_scalar(r_m[:, 0:512], pb[5][0:1, 0:512], 1.0 / C, None, ALU.mult),
                    lambda en: en.tensor_scalar(r_m[:, 512:1024], pb[5][32:33, 0:512], 1.0 / C, None, ALU.mult),
                    lambda en: en.tensor_scalar(r_a[:, 0:512], pb[5][64:65, 0:512], 1.0 / C, None, ALU.mult),
                    lambda en: en.tensor_scalar(r_a[:, 512:1024], pb[2][0:1, 0:512], 1.0 / C, None, ALU.mult),
                    lambda en: en.tensor_mul(r_b, r_m, r_m),
                    lambda en: en.tensor_sub(r_a, r_a, r_b),
                ])
                em("vector", lambda en: en.tensor_scalar(r_a, r_a, EPS, None, ALU.add))
                return mark("vector", "V")
            wait("vector", "T", xb_T[NCT - 1])
            V_ln1a = ln_rows()
            wait("scalar", "V", V_ln1a)
            em("scalar", lambda en: en.activation(r_b, r_a, AF.Sqrt))
            S_sq1 = mark("scalar", "S")
            wait("vector", "S", S_sq1)
            em("vector", lambda en: en.reciprocal(r_b, r_b))
            V_ln1 = mark("vector", "V")

            def bcast(row_ap, dst, tw, marker_prev=None):
                wait("tensor", "V", tw)
                last = None
                for ch in range(2):
                    em("tensor", lambda en, c=ch, r=row_ap: en.matmul(
                        pb[4][:, 0:512], orow[:, :], r[:, 512 * c:512 * (c + 1)], start=True, stop=True))
                    Tb = mark("tensor", "T")
                    wait("vector", "T", Tb)
                    em("vector", lambda en, c=ch, d=dst: en.tensor_copy(d[:, 512 * c:512 * (c + 1)], pb[4][:, 0:512]))
                    last = mark("vector", "V")
                    if ch == 0:
                        wait("tensor", "V", last)
                return last

            bcast(r_m, bcA, V_ln1)
            V_bc1 = bcast(r_rstd, bcB, cnt["V"])

            ckpt(3)
            ap_T = {-2: xb_T[NCT - 2], -1: xb_T[NCT - 1]}
            S_ap = {}
            for ct in range(NCT):
                wait("gpsimd", "S", S_ap.get(ct - 2, 0))
                if ct < 2:
                    wait("gpsimd", "T", xb_T[NCT - 2 + ct])
                em("gpsimd", lambda en, i=ct % 2, c=ct, sm=sB[ct % 2]: en.dma_start(
                    xbs[:, NT * i:NT * (i + 1)],
                    bass.AP(d_xT, c * 128 * NT, [[NT, 128], [1, NT]])).then_inc(sm, 16))
                Gx = bump(f"B{ct % 2}", 16)
                wait("vector", f"B{ct % 2}", Gx)
                if ct > 0:
                    wait("vector", "S", S_ap[ct - 1])
                em("vector", lambda en, i=ct % 2: en.tensor_sub(u2s[:, :], xbs[:, NT * i:NT * (i + 1)], bcA[:, :]))
                em("vector", lambda en: en.tensor_mul(u2s[:, :], u2s[:, :], bcB[:, :]))
                em("vector", lambda en, c=ct: en.tensor_copy(ukcol[:, c:c + 1], u2s[:, 0:1]))
                Vu = mark("vector", "V")
                wait("scalar", "V", Vu)
                if ct == 0:
                    wait("scalar", "V", V_cols)
                em("scalar", lambda en, c=ct: en.activation(
                    zbuf[:, XV0 + NT * c:XV0 + NT * (c + 1)], u2s[:, :], AF.Identity,
                    bias=BV[:, c:c + 1], scale=AV[:, c:c + 1]))
                em("scalar", lambda en, c=ct: en.activation(
                    zbuf[:, XR0 + NT * c:XR0 + NT * (c + 1)], u2s[:, :], AF.Identity,
                    bias=BR[:, c:c + 1], scale=AR[:, c:c + 1]))
                S_ap[ct] = mark("scalar", "S")
            S_ln1apply = cnt["S"]
            wait("vector", "S", S_ln1apply)
            vchain([lambda en: en.tensor_mul(t0c, ukcol, AK),
                    lambda en: en.tensor_add(t0c, t0c, BK)])
            em("vector", lambda en: en.tensor_copy(xk0b[:, :], t0c))
            V_xk0 = mark("vector", "V")

            ckpt(4)
            blk_list = []
            for ot in range(NCT):
                blk_list += [(d_wvT, ot), (d_wrT, ot)]
            blk_list += [(d_wkT, ot) for ot in range(NCT)]
            blk_list += [(d_woT, ot) for ot in range(NCT)]

            def wblk_ap(dram, ot):
                return bass.AP(dram, ot * 128 * C, [[C, 128], [1, C]])

            blk_T = {}
            G_blk = {}

            def emit_blk_dma(bi):
                if bi >= len(blk_list):
                    return
                dram, ot = blk_list[bi]
                if bi < 2:
                    wait("gpsimd", "T", T_mean)
                else:
                    wait("gpsimd", "T", blk_T[bi - 2])
                em("gpsimd", lambda en, i=bi % 2, d=dram, o=ot, sm=sA[bi % 2]: en.dma_start(
                    wq[:, 2048 * i:2048 * (i + 1)], wblk_ap(d, o)).then_inc(sm, 16))
                G_blk[bi] = (f"A{bi % 2}", bump(f"A{bi % 2}", 16))

            emit_blk_dma(0)
            emit_blk_dma(1)

            bi = 0
            for ot in range(NCT):
                for kind, pbase, src0 in (("v", 0, XV0), ("r", 2, XR0)):
                    wait("tensor", G_blk[bi][0], G_blk[bi][1])
                    if bi == 0:
                        wait("tensor", "S", S_ln1apply)
                    for ch in range(2):
                        for ct in range(NCT):
                            em("tensor", lambda en, i=bi % 2, c=ct, cc=ch, pp=pbase, s0=src0: en.matmul(
                                pb[pp + cc][:, 0:512],
                                wq[:, 2048 * i + 128 * c:2048 * i + 128 * (c + 1)],
                                zbuf[:, s0 + NT * c + 512 * cc:s0 + NT * c + 512 * (cc + 1)],
                                start=(c == 0), stop=(c == NCT - 1)))
                    blk_T[bi] = mark("tensor", "T")
                    bi += 1
                    emit_blk_dma(bi + 1)
                wait("scalar", "T", blk_T[bi - 2])
                em("scalar", lambda en: en.activation(vt[:, 0:512], pb[0][:, 0:512], AF.Identity))
                em("scalar", lambda en: en.activation(vt[:, 512:1024], pb[1][:, 0:512], AF.Identity))
                S_v = mark("scalar", "S")
                wait("scalar", "T", blk_T[bi - 1])
                em("scalar", lambda en: en.activation(vt[:, NT:NT + 512], pb[2][:, 0:512], AF.Sigmoid))
                em("scalar", lambda en: en.activation(vt[:, NT + 512:2 * NT], pb[3][:, 0:512], AF.Sigmoid))
                S_r = mark("scalar", "S")
                wait("vector", "S", S_r)
                em("vector", lambda en, o=ot: en.tensor_copy(v0c[:, o:o + 1], vt[:, 0:1]))
                em("vector", lambda en, o=ot: en.tensor_copy(r0c[:, o:o + 1], vt[:, NT:NT + 1]))
                em("vector", lambda en, o=ot: en.tensor_mul(
                    zbuf[:, M0 + NT * o:M0 + NT * (o + 1)], vt[:, 0:NT], vt[:, NT:2 * NT]))
                V_m = mark("vector", "V")
                wait("tensor", "S", S_r)
                wait("tensor", "V", V_m)

            for ot in range(NCT):
                bidx = 32 + ot
                wait("tensor", G_blk[bidx][0], G_blk[bidx][1])
                if ot == 0:
                    wait("tensor", "V", V_xk0)
                for ct in range(NCT):
                    em("tensor", lambda en, i=bidx % 2, c=ct, o=ot: en.matmul(
                        pb[6][:, o:o + 1],
                        wq[:, 2048 * i + 128 * c:2048 * i + 128 * (c + 1)],
                        xk0b[:, c:c + 1],
                        start=(c == 0), stop=(c == NCT - 1)))
                blk_T[bidx] = mark("tensor", "T")
                emit_blk_dma(bidx + 2)
            T_kcol = cnt["T"]
            wait("vector", "T", T_kcol)
            vchain([lambda en: en.tensor_copy(kcolv, pb[6][:, 0:16]),
                    lambda en: en.tensor_add(kcolv, kcolv, tfc),
                    lambda en: en.tensor_max(t0c, kcolv, ppc),
                    lambda en: en.tensor_sub(kcolv, kcolv, t0c)])
            em("vector", lambda en: en.tensor_sub(t0c, ppc, t0c))
            V_pre = mark("vector", "V")
            wait("scalar", "V", V_pre)
            KE1 = kc[:, 0:16]
            em("scalar", lambda en: en.activation(KE1, t0c, AF.Exp))
            flush("scalar", "S")
            em("scalar", lambda en: en.activation(t0c, kcolv, AF.Exp))
            S_exp = mark("scalar", "S")
            wait("vector", "S", S_exp)
            KN = kc[:, 16:32]
            vchain([lambda en: en.tensor_mul(KN, KE1, aac),
                    lambda en: en.tensor_mul(kcolv, t0c, v0c),
                    lambda en: en.tensor_add(KN, KN, kcolv),
                    lambda en: en.tensor_mul(kcolv, KE1, bbc),
                    lambda en: en.tensor_add(kcolv, kcolv, t0c),
                    lambda en: en.reciprocal(kcolv, kcolv),
                    lambda en: en.tensor_mul(KN, KN, kcolv),
                    lambda en: en.tensor_mul(KN, KN, r0c)])
            em("vector", lambda en: en.tensor_copy(bass.AP(zbuf, M0, [[NG * NT, 128], [NT, 16]]), KN))
            V_m0 = mark("vector", "V")

            xr_S = {-2: 0, -1: 0}
            for ot in range(NCT):
                wait("sync", "S", xr_S[ot - 2])
                if ot == 0:
                    wait("sync", "V", V_msel)
                em("sync", lambda en, i=ot % 2, o=ot, sm=sX[ot % 2]: en.dma_start(
                    xcol[:, NT * i:NT * (i + 1)],
                    bass.AP(d_xT, o * 128 * NT, [[NT, 128], [1, NT]])).then_inc(sm, 16))
                Lx = (f"X{ot % 2}", bump(f"X{ot % 2}", 16))
                bidx = 48 + ot
                wait("tensor", G_blk[bidx][0], G_blk[bidx][1])
                if ot == 0:
                    wait("tensor", "V", V_m0)
                for ch in range(2):
                    for mt in range(NCT):
                        em("tensor", lambda en, i=bidx % 2, m_=mt, cc=ch: en.matmul(
                            pb[cc][:, 0:512],
                            wq[:, 2048 * i + 128 * m_:2048 * i + 128 * (m_ + 1)],
                            zbuf[:, M0 + NT * m_ + 512 * cc:M0 + NT * m_ + 512 * (cc + 1)],
                            start=(m_ == 0), stop=(m_ == NCT - 1)))
                blk_T[bidx] = mark("tensor", "T")
                emit_blk_dma(bidx + 2)
                wait("vector", "T", blk_T[bidx])
                wait("vector", Lx[0], Lx[1])
                for ch in range(2):
                    em("vector", lambda en, i=ot % 2, cc=ch, o=ot: en.scalar_tensor_tensor(
                        xcol[:, NT * i + 512 * cc:NT * i + 512 * (cc + 1)],
                        pb[cc][:, 0:512], GA2[:, o:o + 1],
                        xcol[:, NT * i + 512 * cc:NT * i + 512 * (cc + 1)],
                        ALU.mult, ALU.add))
                V_x2 = mark("vector", "V")
                wait("tensor", "V", V_x2)
                wait("scalar", "V", V_x2)
                em("scalar", lambda en, i=ot % 2: en.activation(vt[:, 0:NT], xcol[:, NT * i:NT * (i + 1)], AF.Identity))
                em("scalar", lambda en: en.activation(vt[:, NT:2 * NT], vt[:, 0:NT], AF.Square))
                S_x2c = mark("scalar", "S")
                xr_S[ot] = S_x2c
                wait("tensor", "S", S_x2c)
                for ch in range(2):
                    em("tensor", lambda en, c=ch, st=(ot == 0), sp=(ot == NCT - 1): en.matmul(
                        pb[5][32 * c:32 * c + 1, 0:512], onescol_b,
                        vt[:, 512 * c:512 * (c + 1)], start=st, stop=sp))
                    em("tensor", lambda en, c=ch, st=(ot == 0), sp=(ot == NCT - 1): en.matmul(
                        pb[5 if c == 0 else 2][64 * (1 - c):64 * (1 - c) + 1, 0:512], onescol_b,
                        vt[:, NT + 512 * c:NT + 512 * (c + 1)], start=st, stop=sp))
                T_s2 = mark("tensor", "T")
                wait("sync", "V", V_x2)
                em("sync", lambda en, i=ot % 2, o=ot: en.dma_start(
                    bass.AP(d_x2, o * 128 * NT, [[NT, 128], [1, NT]]),
                    xcol[:, NT * i:NT * (i + 1)]).then_inc(sO, 16))
                bump("O", 16)
                wait("scalar", "T", T_s2)
            T_ln2stats = cnt["T"]
            O_x2 = cnt["O"]

            if _DEBUG:
                wait("gpsimd", "V", V_m0)
                em("gpsimd", lambda en: en.dma_start(dbg["dbg_m"][:, :], zbuf[:, M0:M0 + NCT * NT]).then_inc(sD, 16))
                bump("D", 16)
                em("gpsimd", lambda en: en.dma_start(dbg["dbg_xv"][:, :], zbuf[:, XV0:XV0 + NCT * NT]).then_inc(sD, 16))
                bump("D", 16)
                em("gpsimd", lambda en: en.dma_start(dbg["dbg_mod"][:, :], msel[:, :]).then_inc(sD, 16))
                bump("D", 16)
                em("gpsimd", lambda en: en.dma_start(dbg["dbg_cols"][:, 0:176], pcol[:, :]).then_inc(sD, 16))
                bump("D", 16)

            wait("vector", "T", T_ln2stats)
            V_ln2a = ln_rows()
            wait("scalar", "V", V_ln2a)
            em("scalar", lambda en: en.activation(r_b, r_a, AF.Sqrt))
            S_sq2 = mark("scalar", "S")
            wait("vector", "S", S_sq2)
            em("vector", lambda en: en.reciprocal(r_b, r_b))
            V_ln2 = mark("vector", "V")
            bcast(r_m, bcA, V_ln2)
            V_bc2 = bcast(r_rstd, bcB, cnt["V"])

            ckpt(5)
            x2_V = {-2: 0, -1: 0}
            for p in range(2):
                for ct in range(NCT):
                    idx = p * NCT + ct
                    wait("sync", "S", x2_V[idx - 2])
                    if idx < 2:
                        wait("sync", "S", xr_S[NCT - 2 + idx])
                    if idx == 0:
                        wait("sync", "O", O_x2)
                    em("sync", lambda en, i=idx % 2, o=ct, sm=sX[idx % 2]: en.dma_start(
                        xcol[:, NT * i:NT * (i + 1)],
                        bass.AP(d_x2, o * 128 * NT, [[NT, 128], [1, NT]])).then_inc(sm, 16))
                    Lx = bump(f"X{idx % 2}", 16)
                    wait("vector", f"X{idx % 2}", Lx)
                    em("vector", lambda en, i=idx % 2: en.tensor_sub(u2s[:, :], xcol[:, NT * i:NT * (i + 1)], bcA[:, :]))
                    em("vector", lambda en: en.tensor_mul(u2s[:, :], u2s[:, :], bcB[:, :]))
                    Vu = mark("vector", "V")
                    wait("scalar", "V", Vu)
                    em("scalar", lambda en, c=ct, fn=(AF.Abs if p == 0 else AF.Identity): en.activation(
                        vt[:, 0:NT], u2s[:, :], fn, bias=SBC[:, c:c + 1], scale=CB1[:, c:c + 1]))
                    Sh = mark("scalar", "S")
                    x2_V[idx] = Sh
                    wait("vector", "S", Sh)
                    if p == 0:
                        if ct == 0:
                            em("vector", lambda en: en.tensor_copy(mxa[:, :], vt[:, 0:NT]))
                        else:
                            em("vector", lambda en: en.tensor_max(mxa[:, :], mxa[:, :], vt[:, 0:NT]))
                        mark("vector", "V")
                    else:
                        em("vector", lambda en: en.tensor_mul(u2s[:, :], vt[:, 0:NT], sc1b[:, :]))
                        em("vector", lambda en: en.tensor_scalar(u2s[:, :], u2s[:, :], cm_col, cm_col, ALU.add, ALU.subtract))
                        em("vector", lambda en, c=ct: en.tensor_scalar(
                            h2q[:, NT * c:NT * (c + 1)], u2s[:, :], 127.0, None, ALU.min))
                        mark("vector", "V")
                if p == 0:
                    V_mx = cnt["V"]
                    wait("tensor", "V", V_mx)
                    for j in range(8):
                        em("tensor", lambda en, jj=j: en.transpose(
                            pb[6].bitcast(BF16)[:, 128 * (jj % 4):128 * (jj % 4) + 128],
                            mxa[:, 128 * jj:128 * (jj + 1)], identb))
                        Tt = mark("tensor", "T")
                        wait("vector", "T", Tt)
                        em("vector", lambda en, jj=j: en.reduce_max(
                            t0c[:, jj:jj + 1], pb[6].bitcast(BF16)[:, 128 * (jj % 4):128 * (jj % 4) + 128], AX.X))
                        Vt = mark("vector", "V")
                        wait("tensor", "V", Vt)
                    for j in range(8):
                        em("tensor", lambda en, jj=j: en.transpose(
                            pb[4][0:1, 128 * (jj % 4):128 * (jj % 4) + 128],
                            t0c[:, jj:jj + 1], identf))
                        Tt = mark("tensor", "T")
                        wait("vector", "T", Tt)
                        em("vector", lambda en, jj=j: en.tensor_copy(
                            r_a[:, 128 * jj:128 * (jj + 1)], pb[4][0:1, 128 * (jj % 4):128 * (jj % 4) + 128]))
                        Vt = mark("vector", "V")
                        wait("tensor", "V", Vt)
                    wait("vector", "V", cnt["V"])
                    vchain([lambda en: en.tensor_scalar(r_a, r_a, EPS, None, ALU.max),
                            lambda en: en.reciprocal(r_b, r_a),
                            lambda en: en.tensor_scalar(r_b, r_b, 128.0, None, ALU.mult)])
                    em("vector", lambda en: en.tensor_scalar(r_a, r_a, 1.0 / 128.0, None, ALU.mult))
                    V_s1 = mark("vector", "V")
                    bcast(r_b, sc1b, V_s1)
            V_h2q = cnt["V"]

            if _DEBUG:
                wait("gpsimd", "V", V_h2q)
                em("gpsimd", lambda en: en.dma_start(dbg["dbg_h2q"][:, :], h2q[:, :]).then_inc(sD, 16))
                bump("D", 16)

            u2row = u2s[0:1, 0:1024]
            wait("tensor", "V", V_h2q)
            for j in range(8):
                em("tensor", lambda en, jj=j: en.transpose(
                    pb[4][0:1, 128 * (jj % 4):128 * (jj % 4) + 128], t0c[:, jj:jj + 1], identf))
                Tt = mark("tensor", "T")
                wait("vector", "T", Tt)
                em("vector", lambda en, jj=j: en.tensor_copy(
                    u2row[:, 128 * jj:128 * (jj + 1)], pb[4][0:1, 128 * (jj % 4):128 * (jj % 4) + 128]))
                Vt = mark("vector", "V")
                wait("tensor", "V", Vt)
            wait("vector", "V", cnt["V"])
            vchain([lambda en: en.tensor_scalar(u2row, u2row, EPS, None, ALU.max)])
            em("vector", lambda en: en.tensor_scalar(u2row, u2row, 1.0 / 128.0, None, ALU.mult))
            V_i1r = mark("vector", "V")
            V_i1 = bcast(u2row, bcA, V_i1r)
            wait("scalar", "V", V_i1)
            em("scalar", lambda en: en.activation(bcB[:, :], bcA[:, :], AF.Identity, scale=mcols[:, 0:1]))
            S_i1g = mark("scalar", "S")

            ckpt(6)
            q_T = {-2: 0, -1: 0}
            blkid = 0
            for it in range(NG):
                for wsel, dram, scol_ap, pb_pair in (("g", d_wgT, mcols[:, 4:5], (0, 1)),
                                                     ("u", d_wuT, mcols[:, 5:6], (2, 3))):
                    for half in range(2):
                        slot = blkid % 2
                        wait("sync", "T", q_T[blkid - 2])
                        em("sync", lambda en, s=slot, d=dram, i=it, h=half, sm=sW[slot]: en.dma_start(
                            wf[:, 1024 * s:1024 * (s + 1)],
                            bass.AP(d, (i * 2 + h) * 128 * 1024, [[1024, 128], [1, 1024]])).then_inc(sm, 16))
                        Lw = bump(f"W{slot}", 16)
                        wait("scalar", f"W{slot}", Lw)
                        if blkid == 0:
                            wait("scalar", "V", V_h2q)
                            wait("scalar", "S", S_i1g)
                        em("scalar", lambda en, s=slot, sc=scol_ap: en.activation(
                            wf[:, 1024 * s:1024 * (s + 1)], wf[:, 1024 * s:1024 * (s + 1)],
                            AF.Identity, bias=cm_col, scale=sc))
                        Sq = mark("scalar", "S")
                        wait("vector", "S", Sq)
                        em("vector", lambda en, s=slot: en.tensor_scalar(
                            wf[:, 1024 * s:1024 * (s + 1)], wf[:, 1024 * s:1024 * (s + 1)],
                            cm_col, 1.0, ALU.subtract, ALU.min))
                        em("vector", lambda en, s=slot: en.tensor_scalar(
                            wq[:, 2048 * s:2048 * s + 1024], wf[:, 1024 * s:1024 * (s + 1)],
                            -1.0, None, ALU.max))
                        Vq = mark("vector", "V")
                        wait("tensor", "V", Vq)
                        for ch in range(2):
                            for cti in range(8):
                                ct = 8 * half + cti
                                em("tensor", lambda en, s=slot, c=ct, ci=cti, cc=ch, pp=pb_pair,
                                   st=(half == 0 and cti == 0),
                                   sp=(half == 1 and cti == 7): en.matmul(
                                    pb[pp[cc]][:, 0:512],
                                    wq[:, 2048 * s + 128 * ci:2048 * s + 128 * (ci + 1)],
                                    h2q[:, NT * c + 512 * cc:NT * c + 512 * (cc + 1)],
                                    start=st, stop=sp))
                        q_T[blkid] = mark("tensor", "T")
                        blkid += 1
                wait("vector", "T", q_T[blkid - 1])
                if it == 0:
                    wait("vector", "S", S_i1g)
                for ch in range(2):
                    em("vector", lambda en, cc=ch: en.tensor_mul(
                        u2s[:, 512 * cc:512 * (cc + 1)], pb[cc][:, 0:512], bcB[:, 512 * cc:512 * (cc + 1)]))
                V_g = mark("vector", "V")
                wait("scalar", "V", V_g)
                em("scalar", lambda en: en.activation(vt[:, 0:NT], u2s[:, :], AF.Silu))
                em("scalar", lambda en: en.activation(vt[:, NT:NT + 512], pb[2][:, 0:512], AF.Identity))
                em("scalar", lambda en: en.activation(vt[:, NT + 512:2 * NT], pb[3][:, 0:512], AF.Identity))
                S_z = mark("scalar", "S")
                wait("vector", "S", S_z)
                em("vector", lambda en, i=it: en.tensor_mul(
                    zbuf[:, NT * i:NT * (i + 1)], vt[:, 0:NT], vt[:, NT:2 * NT]))
                V_z = mark("vector", "V")
                wait("tensor", "V", V_z)
                wait("tensor", "S", S_z)
            V_zdone = cnt["V"]

            if _DEBUG:
                wait("gpsimd", "V", V_zdone)
                em("gpsimd", lambda en: en.dma_start(dbg["dbg_z"][:, :], zbuf[:, 0:16 * NT]).then_inc(sD, 16))
                bump("D", 16)

            ckpt(7)
            for g in range(NG):
                for ch in range(2):
                    em("tensor", lambda en, gg=g, cc=ch: en.matmul(
                        pb[cc][:, 0:512], h128b,
                        zbuf[:, NT * gg + 512 * cc:NT * gg + 512 * (cc + 1)],
                        start=True, stop=True))
                Tg = mark("tensor", "T")
                if g % 2 == 0:
                    wait("vector", "T", Tg)
                    em("vector", lambda en, gg=g: en.tensor_copy(zbuf[:, NT * gg:NT * gg + 512], pb[0][:, 0:512]))
                    em("vector", lambda en, gg=g: en.tensor_copy(zbuf[:, NT * gg + 512:NT * (gg + 1)], pb[1][:, 0:512]))
                    Ve = mark("vector", "V")
                    wait("tensor", "V", Ve)
                else:
                    wait("scalar", "T", Tg)
                    em("scalar", lambda en, gg=g: en.activation(zbuf[:, NT * gg:NT * gg + 512], pb[0][:, 0:512], AF.Identity))
                    em("scalar", lambda en, gg=g: en.activation(zbuf[:, NT * gg + 512:NT * (gg + 1)], pb[1][:, 0:512], AF.Identity))
                    Se = mark("scalar", "S")
                    wait("tensor", "S", Se)

            NGP = NT // 8
            for gp in range(NGP):
                for q in range(4):
                    tb = gp * 4 + q
                    em("tensor", lambda en, qq=q, tt=2 * tb: en.matmul(
                        pb[6][0:64, 128 * qq:128 * (qq + 1)],
                        bass.AP(zbuf, tt, [[NG * NT, 128], [NT, NG]]),
                        identb,
                        start=(qq == 0), stop=(qq == 3)))
                    em("tensor", lambda en, qq=q, tt=2 * tb + 1: en.matmul(
                        pb[3][64:128, 128 * qq:128 * (qq + 1)],
                        bass.AP(zbuf, tt, [[NG * NT, 128], [NT, NG]]),
                        identb,
                        start=(qq == 0), stop=(qq == 3),
                        tile_position=(0, 64)))
                T2 = mark("tensor", "T")
                slot = gp % 2
                if gp % 2 == 0:
                    wait("vector", "T", T2)
                    em("vector", lambda en, s=slot: en.tensor_copy(w2b[0:64, 512 * s:512 * (s + 1)], pb[6][0:64, 0:512]))
                    em("vector", lambda en, s=slot: en.tensor_copy(w2b[64:128, 512 * s:512 * (s + 1)], pb[3][64:128, 0:512]))
                    E2 = mark("vector", "V"); E2k = "V"
                else:
                    wait("scalar", "T", T2)
                    em("scalar", lambda en, s=slot: en.activation(w2b[0:64, 512 * s:512 * (s + 1)], pb[6][0:64, 0:512], AF.Identity))
                    em("scalar", lambda en, s=slot: en.activation(w2b[64:128, 512 * s:512 * (s + 1)], pb[3][64:128, 0:512], AF.Identity))
                    E2 = mark("scalar", "S"); E2k = "S"
                wait("tensor", E2k, E2)
                for q in range(4):
                    em("tensor", lambda en, qq=q, s=slot: en.matmul(
                        pb[7][:, 128 * qq:128 * (qq + 1)],
                        w2b[:, 512 * s + 128 * qq:512 * s + 128 * (qq + 1)],
                        bmatb,
                        start=(qq == 0), stop=(qq == 3)))
                T3 = mark("tensor", "T")
                dst = bass.AP(zbuf, 8 * gp, [[NG * NT, 128], [2, 4], [1, 2], [NT, NG]])
                src = bass.AP(pb[7], 0, [[512, 128], [128, 4], [64, 2], [1, 64]])
                if gp % 2 == 0:
                    wait("scalar", "T", T3)
                    em("scalar", lambda en, d=dst, s=src: en.activation(d, s, AF.Identity))
                    E3 = mark("scalar", "S"); E3k = "S"
                else:
                    wait("vector", "T", T3)
                    em("vector", lambda en, d=dst, s=src: en.tensor_copy(d, s))
                    E3 = mark("vector", "V"); E3k = "V"
                wait("tensor", E3k, E3)
            V_fw = cnt["V"]
            S_fw = cnt["S"]

            ckpt(8)
            wait("vector", "S", S_fw)
            mnn = vt[:, 0:NT]
            em("vector", lambda en: en.tensor_copy(mxa[:, :], zbuf[:, 0:NT]))
            em("vector", lambda en: en.tensor_copy(mnn, zbuf[:, 0:NT]))
            for g in range(1, NG):
                em("vector", lambda en, gg=g: en.tensor_tensor(
                    mxa[:, :], mxa[:, :], zbuf[:, NT * gg:NT * (gg + 1)], ALU.max))
                em("vector", lambda en, gg=g: en.tensor_tensor(
                    mnn, mnn, zbuf[:, NT * gg:NT * (gg + 1)], ALU.min))
            em("vector", lambda en: en.scalar_tensor_tensor(mxa[:, :], mnn, -1.0, mxa[:, :], ALU.mult, ALU.max))
            V_mx2 = mark("vector", "V")
            wait("tensor", "V", V_mx2)
            for j in range(8):
                em("tensor", lambda en, jj=j: en.transpose(
                    pb[6].bitcast(BF16)[:, 128 * (jj % 4):128 * (jj % 4) + 128], mxa[:, 128 * jj:128 * (jj + 1)], identb))
                Tt = mark("tensor", "T")
                wait("vector", "T", Tt)
                em("vector", lambda en, jj=j: en.reduce_max(
                    t0c[:, jj:jj + 1], pb[6].bitcast(BF16)[:, 128 * (jj % 4):128 * (jj % 4) + 128], AX.X))
                Vt = mark("vector", "V")
                wait("tensor", "V", Vt)
            for j in range(8):
                em("tensor", lambda en, jj=j: en.transpose(
                    pb[4][0:1, 128 * (jj % 4):128 * (jj % 4) + 128], t0c[:, jj:jj + 1], identf))
                Tt = mark("tensor", "T")
                wait("vector", "T", Tt)
                em("vector", lambda en, jj=j: en.tensor_copy(
                    r_a[:, 128 * jj:128 * (jj + 1)], pb[4][0:1, 128 * (jj % 4):128 * (jj % 4) + 128]))
                Vt = mark("vector", "V")
                wait("tensor", "V", Vt)
            wait("vector", "V", cnt["V"])
            vchain([lambda en: en.tensor_mul(r_b, r_a, bcA[0:1, 0:NT]),
                    lambda en: en.tensor_scalar(r_b, r_b, mcols[0:1, 1:2], CNORM, ALU.mult, ALU.mult),
                    lambda en: en.tensor_scalar(r_b, r_b, EPS, None, ALU.max),
                    lambda en: en.tensor_scalar(r_m, r_b, 1.0 / 128.0, None, ALU.mult),
                    lambda en: en.reciprocal(r_a, r_b),
                    lambda en: en.tensor_mul(r_a, r_a, bcA[0:1, 0:NT])])
            em("vector", lambda en: en.tensor_scalar(r_a, r_a, mcols[0:1, 1:2], 128.0 * CNORM, ALU.mult, ALU.mult))
            V_q2r = mark("vector", "V")
            QBb = wq[:, 0:1024]
            IBb = bcA
            wait("tensor", "V", V_q2r)
            last = None
            for ch in range(2):
                em("tensor", lambda en, c=ch: en.matmul(
                    pb[4][:, 0:512], orow[:, :], r_a[:, 512 * c:512 * (c + 1)], start=True, stop=True))
                Tb = mark("tensor", "T")
                wait("vector", "T", Tb)
                em("vector", lambda en, c=ch: en.tensor_copy(QBb[:, 512 * c:512 * (c + 1)], pb[4][:, 0:512]))
                Vb = mark("vector", "V")
                wait("tensor", "V", Vb)
                em("tensor", lambda en, c=ch: en.matmul(
                    pb[4][:, 0:512], orow[:, :], r_m[:, 512 * c:512 * (c + 1)], start=True, stop=True))
                Tb2 = mark("tensor", "T")
                wait("vector", "T", Tb2)
                em("vector", lambda en, c=ch: en.tensor_copy(IBb[:, 512 * c:512 * (c + 1)], pb[4][:, 0:512]))
                last = mark("vector", "V")
                wait("tensor", "V", last)
            for g in range(NG):
                em("vector", lambda en, gg=g: en.tensor_mul(u2s[:, :], zbuf[:, NT * gg:NT * (gg + 1)], QBb[:, :]))
                Vm = mark("vector", "V")
                wait("scalar", "V", Vm)
                em("scalar", lambda en: en.activation(u2s[:, :], u2s[:, :], AF.Identity, bias=cm_col))
                Sa = mark("scalar", "S")
                wait("vector", "S", Sa)
                em("vector", lambda en, gg=g: en.tensor_scalar(
                    zbuf[:, NT * gg:NT * (gg + 1)], u2s[:, :], cm_col, 127.0, ALU.subtract, ALU.min))
                mark("vector", "V")
            V_xq = cnt["V"]

            if _DEBUG:
                wait("gpsimd", "V", V_xq)
                em("gpsimd", lambda en: en.dma_start(dbg["dbg_xq"][:, :], zbuf[:, 0:16 * NT]).then_inc(sD, 16))
                bump("D", 16)

            ckpt(9)
            d_T = {-2: 0, -1: 0}
            dblk = 0
            xo_V = {-2: 0, -1: 0}
            for ot in range(NCT):
                wait("sync", "V", xo_V[ot - 2])
                if ot == 0:
                    wait("sync", "V", V_xq)
                em("sync", lambda en, i=ot % 2, o=ot, sm=sX[ot % 2]: en.dma_start(
                    xcol[:, NT * i:NT * (i + 1)],
                    bass.AP(d_x2, o * 128 * NT, [[NT, 128], [1, NT]])).then_inc(sm, 16))
                Lx2 = bump(f"X{ot % 2}", 16)
                Td = None
                for sub in range(8):
                    slot = dblk % 2
                    wait("sync", "T", d_T[dblk - 2])
                    em("sync", lambda en, s=slot, o=ot, sb_=sub, sm=sW[slot]: en.dma_start(
                        wf[:, 1024 * s:1024 * (s + 1)],
                        bass.AP(d_wdT, (o * 8 + sb_) * 128 * 1024, [[1024, 128], [1, 1024]])).then_inc(sm, 16))
                    Lw = bump(f"W{slot}", 16)
                    wait("scalar", f"W{slot}", Lw)
                    if dblk == 0:
                        wait("scalar", "V", V_xq)
                    em("scalar", lambda en, s=slot: en.activation(
                        wf[:, 1024 * s:1024 * (s + 1)], wf[:, 1024 * s:1024 * (s + 1)],
                        AF.Identity, bias=cm_col, scale=mcols[:, 6:7]))
                    Sq = mark("scalar", "S")
                    wait("vector", "S", Sq)
                    em("vector", lambda en, s=slot: en.tensor_scalar(
                        wf[:, 1024 * s:1024 * (s + 1)], wf[:, 1024 * s:1024 * (s + 1)],
                        cm_col, 1.0, ALU.subtract, ALU.min))
                    em("vector", lambda en, s=slot: en.tensor_scalar(
                        wq[:, 2048 + 1024 * s:2048 + 1024 * (s + 1)], wf[:, 1024 * s:1024 * (s + 1)],
                        -1.0, None, ALU.max))
                    Vq = mark("vector", "V")
                    wait("tensor", "V", Vq)
                    for ch in range(2):
                        for gg in range(8):
                            em("tensor", lambda en, s=slot, g_=8 * sub + gg, gi=gg, cc=ch,
                               st=(sub == 0 and gg == 0),
                               sp=(sub == 7 and gg == 7): en.matmul(
                                pb[cc][:, 0:512],
                                wq[:, 2048 + 1024 * s + 128 * gi:2048 + 1024 * s + 128 * (gi + 1)],
                                zbuf[:, NT * g_ + 512 * cc:NT * g_ + 512 * (cc + 1)],
                                start=st, stop=sp))
                    Td = mark("tensor", "T")
                    d_T[dblk] = Td
                    dblk += 1
                wait("vector", "T", Td)
                wait("vector", f"X{ot % 2}", Lx2)
                if ot == 0:
                    wait("vector", "S", S_gbd)
                for ch in range(2):
                    em("vector", lambda en, cc=ch: en.tensor_mul(
                        u2s[:, 512 * cc:512 * (cc + 1)], pb[cc][:, 0:512], IBb[:, 512 * cc:512 * (cc + 1)]))
                for ch in range(2):
                    em("vector", lambda en, i=ot % 2, cc=ch, o=ot: en.scalar_tensor_tensor(
                        xcol[:, NT * i + 512 * cc:NT * i + 512 * (cc + 1)],
                        u2s[:, 512 * cc:512 * (cc + 1)], GBD[:, o:o + 1],
                        xcol[:, NT * i + 512 * cc:NT * i + 512 * (cc + 1)],
                        ALU.mult, ALU.add))
                V_out = mark("vector", "V")
                xo_V[ot] = V_out
                wait("tensor", "V", V_out)
                wait("sync", "V", V_out)
                em("sync", lambda en, i=ot % 2, o=ot: en.dma_start(
                    bass.AP(d_out, o * 128 * NT, [[NT, 128], [1, NT]]),
                    xcol[:, NT * i:NT * (i + 1)]).then_inc(sO, 16))
                bump("O", 16)
            O_final = cnt["O"]
        except _Stop:
            pass
        if cnt["O"]:
            wait("gpsimd", "O", cnt["O"])
        if _DEBUG2:
            for nm, src in [("dbg2_kc", kc[:, :]), ("dbg2_pcol", pcol[:, :]),
                            ("dbg2_mcols", mcols[:, :]), ("dbg2_srow", srow[:, :])]:
                em("gpsimd", lambda en, n_=nm, s_=src: en.dma_start(
                    dbg2[n_][:, :], s_).then_inc(sD, 16))
                bump("D", 16)
            wait("gpsimd", "D", cnt["D"])
        if _DEBUG:
            wait("gpsimd", "D", cnt["D"]) if cnt["D"] else None

        @block.sync
        def _(eng):
            for f in sched["sync"]:
                f(eng)

        @block.gpsimd
        def _(eng):
            for f in sched["gpsimd"]:
                f(eng)

        @block.tensor
        def _(eng):
            for f in sched["tensor"]:
                f(eng)

        @block.vector
        def _(eng):
            for f in sched["vector"]:
                f(eng)

        @block.scalar
        def _(eng):
            for f in sched["scalar"]:
                f(eng)

    return nc


def _hadamard(n):
    H = np.array([[1.0]])
    while H.shape[0] < n:
        H = np.block([[H, H], [H, -H]])
    return H


def _colsof(v):
    return np.ascontiguousarray(np.asarray(v, np.float32).reshape(16, 128).T)


def _pack_attn(wT):
    return np.ascontiguousarray(
        wT.reshape(16, 128, 16, 128).transpose(2, 1, 0, 3).reshape(16 * 128, 16 * 128))


def _pack_gu(wT):
    return np.ascontiguousarray(
        wT.reshape(2, 8, 128, NG, 128).transpose(3, 0, 2, 1, 4).reshape(NG * 2 * 128, 1024))


def _pack_d(wdT):
    return np.ascontiguousarray(
        wdT.reshape(8, 8, 128, 16, 128).transpose(3, 0, 2, 1, 4).reshape(16 * 8 * 128, 1024))


def make_inmaps(inputs):
    f32 = np.float32
    x = np.asarray(inputs["L0_attn_x"], f32)
    wvT = np.ascontiguousarray(np.asarray(inputs["L0_attn_w_value"], f32).T)
    wrT = np.ascontiguousarray(np.asarray(inputs["L0_attn_w_rec"], f32).T)
    wkT = np.ascontiguousarray(np.asarray(inputs["L0_attn_w_key"], f32).T)
    woT = np.ascontiguousarray(np.asarray(inputs["L0_attn_w_out"], f32).T)
    wgT = np.ascontiguousarray(np.asarray(inputs["L0_ffn_w_gate"], f32).T)
    wuT = np.ascontiguousarray(np.asarray(inputs["L0_ffn_w_up"], f32).T)
    wdT = np.ascontiguousarray(np.asarray(inputs["L0_ffn_w_down"], f32).T)
    wmod = np.asarray(inputs["L0_wmod"], f32)
    cT = np.ascontiguousarray(np.asarray(inputs["L0_c"], f32).T)
    wvP = _pack_attn(wvT); wrP = _pack_attn(wrT); wkP = _pack_attn(wkT); woP = _pack_attn(woT)
    wgP = _pack_gu(wgT); wuP = _pack_gu(wuT); wdP = _pack_d(wdT)

    H128 = _hadamard(128)
    H64 = _hadamard(64)
    identb = np.eye(128)
    Bmat = np.zeros((128, 128))
    for d_ in range(2):
        Bmat[d_ * 64:(d_ + 1) * 64, d_ * 64:(d_ + 1) * 64] = H64
    onescol = np.ones((128, 1))
    constb = np.concatenate([identb, H128, Bmat, onescol], axis=1).astype(ml_dtypes.bfloat16)
    constf = np.eye(128, dtype=f32)
    onesrow = np.ones((1, 128), f32)

    tf = _colsof(inputs["L0_attn_time_first"])
    tmk = _colsof(np.asarray(inputs["L0_attn_time_mix_k"]).ravel())
    tmv = _colsof(np.asarray(inputs["L0_attn_time_mix_v"]).ravel())
    tmr = _colsof(np.asarray(inputs["L0_attn_time_mix_r"]).ravel())
    xxc = _colsof(np.asarray(inputs["L0_attn_xx"]).ravel())
    aac = _colsof(np.asarray(inputs["L0_attn_aa"]).ravel())
    bbc = _colsof(np.asarray(inputs["L0_attn_bb"]).ravel())
    ppc = _colsof(np.asarray(inputs["L0_attn_pp"]).ravel())
    zc = np.zeros_like(aac)
    ppn = np.full_like(ppc, -1e38)

    in_maps = []
    for i in range(8):
        b, h = i // 2, i % 2
        xT = np.ascontiguousarray(x[b, h * NT:(h + 1) * NT, :].T)
        pvv = np.concatenate(
            [tf, tmk, tmv, tmr, xxc] + ([aac, bbc, ppc] if h == 0 else [zc, zc, ppn]), axis=1)
        mask = np.zeros((128, NMOD * 4), f32)
        mask[:, b::4] = 1.0
        in_maps.append({
            "xT": xT, "wvT": wvP, "wrT": wrP, "wkT": wkP, "woT": woP,
            "wgT": wgP, "wuT": wuP, "wdT": wdP,
            "wmod_sl": np.ascontiguousarray(wmod[i * NWM:(i + 1) * NWM, :]),
            "cT_sl": np.ascontiguousarray(cT[i * NWM:(i + 1) * NWM, :]),
            "pvecs": np.ascontiguousarray(pvv),
            "constb": constb, "constf": constf, "onesrow": onesrow,
            "maskf": mask,
            "wms_g": np.ascontiguousarray(wgT[:, i * 1024:(i + 1) * 1024]),
            "wms_u": np.ascontiguousarray(wuT[:, i * 1024:(i + 1) * 1024]),
            "wms_d": np.ascontiguousarray(wdT[:, i * 256:(i + 1) * 256]),
        })
    return in_maps


def kernel(**inputs):
    in_maps = make_inmaps(inputs)
    nc = _build()
    res = run_bass_kernel_spmd(nc, in_maps, core_ids=list(range(8)))
    out = np.empty((B, T, C), np.float32)
    for i in range(8):
        b, h = i // 2, i % 2
        out[b, h * NT:(h + 1) * NT, :] = res.results[i]["outT"].T
    return out
